# version 05
# speedup vs baseline: 1.0465x; 1.0017x over previous
import os
import sys
import numpy as np
from contextlib import ExitStack

sys.path.insert(0, "/opt/trn_rl_repo")
import concourse.bass as bass
import concourse.mybir as mybir
from concourse.bass_utils import run_bass_kernel_spmd

F32 = mybir.dt.float32
BF16 = mybir.dt.bfloat16
I32 = mybir.dt.int32
AF = mybir.ActivationFunctionType
ALU = mybir.AluOpType
AX = mybir.AxisListType

NCORES = 8
L = 4096
D = 1024
FF = 4096
NM = 32
Q = 128
NCH = L // Q
NT = L // 128
NG = D // 128
EPS = 1e-5
PI = float(np.pi)

DEV_STAGE = int(os.environ.get("MK_STAGE", "5"))


class Sem:
    def __init__(self, h):
        self.h = h
        self.v = 0

    def inc(self, n=1):
        self.v += n
        return self.v


class Prog:
    def __init__(self):
        self.ops = {k: [] for k in ("sync", "gpsimd", "vector", "scalar", "tensor")}

    def on(self, eng, fn):
        self.ops[eng].append(fn)


def rsqrt_ops(e, out, v, scratch):
    y0, t = scratch[:, 0:1], scratch[:, 1:2]
    vi = v.bitcast(I32)
    yi = y0.bitcast(I32)
    e.tensor_scalar(out=yi, in0=vi, scalar1=1, scalar2=None,
                    op0=ALU.logical_shift_right)
    e.tensor_scalar(out=yi, in0=yi, scalar1=-1, scalar2=0x5F3759DF,
                    op0=ALU.mult, op1=ALU.add)
    for _ in range(3):
        e.tensor_mul(t, y0, y0)
        e.tensor_mul(t, t, v)
        e.tensor_scalar(out=t, in0=t, scalar1=-0.5, scalar2=1.5,
                        op0=ALU.mult, op1=ALU.add)
        e.tensor_mul(y0, y0, t)
    e.tensor_copy(out, y0)


def build_program():
    nc = bass.Bass("TRN2", target_bir_lowering=False,
                   detect_race_conditions=bool(int(os.environ.get("MK_RACE", "0"))))
    P = Prog()

    from concourse.hw_specs import get_activation_tables
    act_sets = {name: i for i, name in enumerate(get_activation_tables(nc.m.arch))}

    def load_act(e, set_name):
        e.add_instruction(mybir.InstLoadActFuncSet(
            name=nc.get_next_instruction_name(),
            act_func_set_id=act_sets[set_name], ins=[], outs=[]))

    dI = lambda n, sh, dt=F32: nc.dram_tensor(n, sh, dt, kind="ExternalInput").ap()
    x_in = dI("x", [L, D])
    in_W = dI("in_W", [D, D])
    in_bB = dI("in_bB", [128, D])
    dt_col = dI("dt_col", [128, 8])
    dsk_col = dI("dsk_col", [128, 8])
    lar_g = dI("lar_g", [128, 256])
    aim_g = dI("aim_g", [128, 256])
    c_re_g = dI("c_re_g", [128, 256])
    c_im_g = dI("c_im_g", [128, 256])
    glu_W = dI("glu_W", [2 * D, D])
    glu_b_col = dI("glu_b_col", [128, 16])
    out_W = dI("out_W", [D, D])
    out_bB = dI("out_bB", [128, D])
    gamma_col = dI("gamma_col", [128, 8])
    beta_col = dI("beta_col", [128, 8])
    fc1_q = dI("fc1_q", [D, FF], I32)
    fc1_scales_b = dI("fc1_scales_b", [8, 128, FF])
    fc1_bias_col = dI("fc1_bias_col", [128, 32])
    fc2_qc = dI("fc2_qc", [8, 128, 32 * 128], I32)
    fc2_scales_c = dI("fc2_scales_c", [8, 128, FF])
    fc2_bias_col = dI("fc2_bias_col", [128, 8])

    out_ext = nc.dram_tensor("out", [L, D], F32, kind="ExternalOutput").ap()

    def dS(name, sh, dt=F32, dbg=False):
        kind = "ExternalOutput" if dbg else "Internal"
        return nc.dram_tensor(name, sh, dt, kind=kind).ap()

    x1_hbm = dS("dbg_x1" if DEV_STAGE == 1 else "x1_hbm", [L, D], F32, DEV_STAGE == 1)
    x2_hbm = dS("dbg_x2" if DEV_STAGE == 4 else "x2_hbm", [L, D], F32, DEV_STAGE == 4)
    uT_hbm = dS("dbg_u" if DEV_STAGE == 2 else "uT_hbm", [128, NCH * D], BF16, DEV_STAGE == 2)

    w1g_hbm = dS("w1g_hbm", [128, NG * FF], BF16)
    w2q_hbm = dS("w2q_hbm", [8, 128, 32 * 128], BF16)
    brow_hbm = dS("brow_hbm", [1, FF])
    TDBG = DEV_STAGE == 25
    kT_hbm = dS("kT_hbm", [32, 256, 32], BF16, TDBG)
    wg_hbm = dS("wg_hbm", [NG, 128, 8192], BF16, TDBG)
    wcre_hbm = dS("wcre_hbm", [128, 32768], BF16, TDBG)
    wcim_hbm = dS("wcim_hbm", [128, 32768], BF16, TDBG)
    lamq_hbm = dS("lamq_hbm", [128, 512], F32, TDBG)
    gluWT_hbm = dS("gluWT_hbm", [128, 16384], BF16, TDBG)
    outWT_hbm = dS("outWT_hbm", [128, 8192], BF16, TDBG)
    yhm_hbm = dS("dbg_y" if DEV_STAGE == 3 else "yhm_hbm", [D, L], BF16, DEV_STAGE == 3)

    es = ExitStack()
    es_p01 = ExitStack()
    SB = lambda n, sh, dt=F32: es.enter_context(nc.sbuf_tensor(n, sh, dt))
    SB1 = lambda n, sh, dt=F32: es_p01.enter_context(nc.sbuf_tensor(n, sh, dt, side="right"))
    PS = lambda n, sh, dt=F32: es.enter_context(nc.psum_tensor(n, sh, dt))

    pA = PS("pA", [128, 512])
    pB = PS("pB", [128, 512])
    pC = PS("pC", [128, 512])
    pD = PS("pD", [128, 512])
    pE = PS("pE", [128, 1024], BF16)
    pF = PS("pF", [128, 512])
    pG = PS("pG", [128, 512])
    pH = PS("pH", [128, 512])

    ident = SB("ident", [128, 128], BF16)
    ident32 = SB("ident32", [128, 128])
    bias_sb = SB("bias_sb", [128, 64])
    b1p_col = SB("b1p_col", [128, 32])
    b2q_col = SB("b2q_col", [128, 8])
    statrow = SB("statrow", [128, 80])
    srow2 = SB("srow2", [128, NT])
    crow2 = SB("crow2", [128, NT])
    glub_sb = SB("glub_sb", [128, 16])

    w1g = SB1("w1g", [128, NG * FF], BF16)
    qbuf = SB1("qbuf", [128, FF])
    sclbuf = SB1("sclbuf", [128, FF])
    wtmp = SB1("wtmp", [128, FF])
    sqscrap = wtmp[:, 0:D]
    xhalf = SB1("xhalf", [128, 4 * D])
    xsbuf = SB1("xsbuf", [128, D], BF16)
    xs_fm = SB1("xs_fm", [128, NG * 512], BF16)
    h2_fm = SB1("h2_fm", [128, 32 * 512], BF16)
    w2buf = SB1("w2buf", [128, 2 * 4096], BF16)
    zfm = SB1("zfm", [128, 512])
    x1part = [qbuf[:, i * D:(i + 1) * D] for i in range(2)] + \
             [sclbuf[:, i * D:(i + 1) * D] for i in range(2)]
    wbf = h2_fm[:, 0:FF]
    brow = xhalf[0:1, 0:FF]

    names = ("ld0", "deq", "w1", "bw", "bwc", "xld", "xfree", "sq", "stats", "xs",
             "xst", "xsfm", "fc1", "h2", "w2ld", "w2free", "fc2", "zev", "zt",
             "ztfree", "x1rdy", "x1st", "ph", "gld", "gdma", "uev", "gmm", "scn",
             "shf", "cnv", "yev", "glu", "op", "vch")
    s = {n: Sem(es.enter_context(nc.semaphore(n))) for n in names}

    ENGS = ("sync", "gpsimd", "vector", "scalar", "tensor")

    def barrier(pre_waits=()):
        for eng, sem_name, val in pre_waits:
            P.on(eng, lambda e, sn=sem_name, v=val: e.wait_ge(s[sn].h, v))
        P.on("vector", lambda e: e.tensor_copy(statrow[:, 76:77],
                                               statrow[:, 2:3]).then_inc(s["ph"].h, 1))
        P.on("scalar", lambda e: e.activation(out=statrow[:, 77:78], in_=statrow[:, 2:3],
                                              func=AF.Identity).then_inc(s["ph"].h, 1))
        P.on("tensor", lambda e: e.matmul(pF[0:1, 0:1], ident32[:, 0:1], ident32[:, 0:1],
                                          start=True, stop=True).then_inc(s["ph"].h, 1))
        P.on("gpsimd", lambda e: e.nop().then_inc(s["ph"].h, 1))
        P.on("sync", lambda e: e.nop().then_inc(s["ph"].h, 1))
        s["ph"].inc(5)
        tgt = s["ph"].v
        for eng in ENGS:
            P.on(eng, lambda e, t=tgt: e.wait_ge(s["ph"].h, t))

    def g_init(e):
        e.memset(ident[:], 0.0).then_inc(s["gld"].h, 1)
        e.memset(ident32[:], 0.0).then_inc(s["gld"].h, 1)
        e.wait_ge(s["gld"].h, 2)
        e.affine_select(out=ident[:], in_=ident[:], compare_op=ALU.not_equal,
                        fill=1.0, base=0, pattern=[[-1, 128]], channel_multiplier=1)
        e.affine_select(out=ident32[:], in_=ident32[:], compare_op=ALU.not_equal,
                        fill=1.0, base=0, pattern=[[-1, 128]], channel_multiplier=1)
        e.nop().then_inc(s["gld"].h, 2)
    P.on("gpsimd", g_init)
    s["gld"].inc(4)
    ident_ready = s["gld"].v

    def sy_small(e):
        e.dma_start(out=bias_sb[:, 0:8], in_=gamma_col).then_inc(s["ld0"].h, 16)
        e.dma_start(out=bias_sb[:, 8:16], in_=beta_col).then_inc(s["ld0"].h, 16)
        e.dma_start(out=bias_sb[:, 16:48], in_=fc1_bias_col).then_inc(s["ld0"].h, 16)
        e.dma_start(out=bias_sb[:, 48:56], in_=fc2_bias_col).then_inc(s["ld0"].h, 16)
        e.dma_start(out=glub_sb[:], in_=glu_b_col).then_inc(s["ld0"].h, 16)
    P.on("sync", sy_small)
    s["ld0"].inc(16 * 5)

    for g in range(NG):
        qsl = qbuf[:]
        ssl = sclbuf[:]
        if g >= 1:
            w_prev = s["w1"].v
            P.on("gpsimd", lambda e, w=w_prev: e.wait_ge(s["w1"].h, w))
            P.on("sync", lambda e, w=w_prev: e.wait_ge(s["w1"].h, w))
        def g_ld(e, qsl=qsl, g=g, wp=s["gdma"].v):
            if wp:
                e.wait_ge(s["gdma"].h, wp)
            e.dma_start(out=qsl, in_=fc1_q[g * 128:(g + 1) * 128, :]).then_inc(s["gdma"].h, 16)
        P.on("gpsimd", g_ld)
        def sy_ld(e, ssl=ssl, g=g):
            e.dma_start(out=ssl, in_=fc1_scales_b[g]).then_inc(s["ld0"].h, 16)
        P.on("sync", sy_ld)
        gq_tgt = s["gdma"].inc(16)
        ld_tgt = s["ld0"].inc(16)

        def v_deq(e, qsl=qsl, ssl=ssl, w=ld_tgt, wg=gq_tgt):
            e.wait_ge(s["ld0"].h, w)
            e.wait_ge(s["gdma"].h, wg)
            e.scalar_tensor_tensor(out=wtmp[:], in0=qsl, scalar=-128.0,
                                   in1=ssl, op0=ALU.add, op1=ALU.mult).then_inc(s["w1"].h, 1)
        w1_tgt = s["w1"].inc()
        P.on("vector", v_deq)

        for fch in range(8):
            bwc_need = s["bwc"].v
            def t_bw(e, g=g, fch=fch, w=w1_tgt, bn=bwc_need):
                if fch == 0:
                    e.wait_ge(s["w1"].h, w)
                if bn > 0:
                    e.wait_ge(s["bwc"].h, bn)
                e.matmul(pF[0:1, :], bias_sb[:, 8 + g:9 + g],
                         wtmp[:, fch * 512:(fch + 1) * 512],
                         start=True, stop=True).then_inc(s["bw"].h, 1)
            P.on("tensor", t_bw)
            bw_tgt = s["bw"].inc()

            def v_bw(e, g=g, fch=fch, w=bw_tgt):
                e.wait_ge(s["bw"].h, w)
                if g == 0:
                    e.tensor_copy(brow[0:1, fch * 512:(fch + 1) * 512],
                                  pF[0:1, :]).then_inc(s["bwc"].h, 1)
                else:
                    e.tensor_add(brow[0:1, fch * 512:(fch + 1) * 512],
                                 brow[0:1, fch * 512:(fch + 1) * 512],
                                 pF[0:1, :]).then_inc(s["bwc"].h, 1)
            P.on("vector", v_bw)
            s["bwc"].inc()

        def sc_gam(e, g=g, w=w1_tgt):
            if g == 0:
                load_act(e, "natural_log_exp_and_others")
            e.wait_ge(s["w1"].h, w)
            e.activation(out=w1g[:, g * FF:(g + 1) * FF], in_=wtmp[:],
                         func=AF.Identity,
                         scale=bias_sb[:, g:g + 1]).then_inc(s["deq"].h, 1)
        P.on("scalar", sc_gam)
        deq_tgt = s["deq"].inc()

        P.on("vector", lambda e, w=deq_tgt: e.wait_ge(s["deq"].h, w))

        def sy_w1out(e, g=g, w=deq_tgt):
            e.wait_ge(s["deq"].h, w)
            e.dma_start(out=w1g_hbm[:, g * FF:(g + 1) * FF],
                        in_=w1g[:, g * FF:(g + 1) * FF]).then_inc(s["ld0"].h, 16)
        P.on("sync", sy_w1out)
        s["ld0"].inc(16)

    def g_rowcol(e, w=s["bwc"].v, wp=s["gdma"].v):
        e.wait_ge(s["bwc"].h, w)
        if wp:
            e.wait_ge(s["gdma"].h, wp)
        e.dma_start(out=brow_hbm, in_=brow).then_inc(s["gdma"].h, 16)
    P.on("gpsimd", g_rowcol)
    st1 = s["gdma"].inc(16)

    brrl = h2_fm[:].bitcast(F32)[:, 4096:4224]
    def g_rowcol2(e, w=st1):
        e.wait_ge(s["gdma"].h, w)
        src = bass.AP(brow_hbm.tensor, brow_hbm.offset, [[128, 32], [1, 128]])
        e.dma_start(out=brrl[0:32, :], in_=src).then_inc(s["gdma"].h, 16)
    P.on("gpsimd", g_rowcol2)
    rc_tgt = s["gdma"].inc(16)

    def t_b1p(e, w=rc_tgt):
        e.wait_ge(s["gdma"].h, w)
        e.transpose(pG[:, 0:128], brrl, ident32[:]).then_inc(s["zt"].h, 1)
    P.on("tensor", t_b1p)
    b1pt_tgt = s["zt"].inc()

    def v_b1p(e, w=b1pt_tgt):
        e.wait_ge(s["zt"].h, w)
        e.tensor_add(b1p_col[:], pG[:, 0:32], bias_sb[:, 16:48])
        e.tensor_scalar_mul(b2q_col[:], bias_sb[:, 48:56],
                            0.25).then_inc(s["stats"].h, 1)
    P.on("vector", v_b1p)
    bias_ready = s["stats"].inc()

    w2q_done = [0] * 8
    for dt_ in range(8):
        qsl = qbuf[:]
        ssl = sclbuf[:]
        w_prev = s["deq"].v
        P.on("gpsimd", lambda e, w=w_prev: e.wait_ge(s["deq"].h, w))
        P.on("sync", lambda e, w=w_prev: e.wait_ge(s["deq"].h, w))
        def g_ld2(e, dt_=dt_, qsl=qsl, wp=s["gdma"].v):
            if wp:
                e.wait_ge(s["gdma"].h, wp)
            e.dma_start(out=qsl, in_=fc2_qc[dt_]).then_inc(s["gdma"].h, 16)
        P.on("gpsimd", g_ld2)
        def sy_ld2(e, dt_=dt_, ssl=ssl):
            e.dma_start(out=ssl, in_=fc2_scales_c[dt_]).then_inc(s["ld0"].h, 16)
        P.on("sync", sy_ld2)
        gq_tgt = s["gdma"].inc(16)
        ld_tgt = s["ld0"].inc(16)

        def v_deq2(e, qsl=qsl, ssl=ssl, w=ld_tgt, wg=gq_tgt):
            e.wait_ge(s["ld0"].h, w)
            e.wait_ge(s["gdma"].h, wg)
            e.scalar_tensor_tensor(out=wtmp[:], in0=qsl, scalar=-128.0,
                                   in1=ssl, op0=ALU.add, op1=ALU.mult)
            e.tensor_scalar_mul(wbf[:], wtmp[:], 0.25).then_inc(s["deq"].h, 1)
        P.on("vector", v_deq2)
        deq_tgt = s["deq"].inc()

        def sy_w2out(e, dt_=dt_, w=deq_tgt):
            e.wait_ge(s["deq"].h, w)
            e.dma_start(out=w2q_hbm[dt_], in_=wbf[:]).then_inc(s["ld0"].h, 16)
        P.on("sync", sy_w2out)
        wb_done = s["ld0"].inc(16)
        w2q_done[dt_] = wb_done
        P.on("vector", lambda e, w=wb_done: e.wait_ge(s["ld0"].h, w))

    if DEV_STAGE >= 3:
        TCUT = int(os.environ.get("MK_TCUT", "4"))
        GCUT = int(os.environ.get("MK_GCUT", "5"))
        TWO_PI = float(np.float32(2 * np.pi))
        INV_2PI = float(np.float32(1.0 / (2 * np.pi)))
        HALF_PI = float(np.float32(np.pi / 2))
        PI32 = float(np.float32(np.pi))
        GR = qbuf[:]
        DT, DSK, MRE, MIM, TAa, TBb, TCc = 0, 8, 16, 32, 48, 64, 80
        E1, AIMG, LRE, LIM, CREo, CIMo = 112, 368, 624, 880, 1136, 1392
        T1o, T2o, T3o, T4o, T5o, T6o = 1648, 1904, 2160, 2416, 2672, 2928
        LQRE, LQIM = 3184, 3440
        DTE, DT2 = 3696, 3704
        LTRE, LTIM, CT2RE, CT2IM = 0, 256, 512, 768
        CTTRE, CTTIM = 1024, 1280
        MTRE, MTIM, MTA, MTB, MTC = 1536, 1568, 1600, 1632, 1664
        LQTRE, LQTIM = 3136, 3392
        scl = sclbuf[:]
        scl_bf = scl.bitcast(BF16)
        kpad = scl_bf[:, 3584:3840]
        ktst = scl_bf[:, 3840:4096]
        wstg1 = scl_bf[:, 4224:5248]
        wstg2 = scl_bf[:, 5248:6272]
        h2f = h2_fm[:].bitcast(F32)
        kb1o, kb2o = 4096, 6144
        wgstage = h2_fm[:, 0:8192]
        PH = [wtmp[:], xhalf[:]]
        wcstage = xs_fm[:, 0:4096]

        gq = lambda off, n=256: GR[:, off:off + n]
        gs = lambda off, n=256: scl[:, off:off + n]

        def VI2(ins):
            s["vch"].inc()
            ins.then_inc(s["vch"].h, 1)

        def VW2(e):
            e.wait_ge(s["vch"].h, s["vch"].v)

        def ap3(base, off, fdims):
            return bass.AP(base.tensor, base.offset + off,
                           [[base.ap[0][0], 128]] + fdims)

        def sy_grids(e, w=s["deq"].v):
            e.wait_ge(s["deq"].h, w)
            e.dma_start(out=GR[:, DT:DT + 8], in_=dt_col).then_inc(s["ld0"].h, 16)
            e.dma_start(out=GR[:, DSK:DSK + 8], in_=dsk_col).then_inc(s["ld0"].h, 16)
            e.dma_start(out=gq(T1o), in_=lar_g).then_inc(s["ld0"].h, 16)
            e.dma_start(out=gq(AIMG), in_=aim_g).then_inc(s["ld0"].h, 16)
            e.dma_start(out=gq(CREo), in_=c_re_g).then_inc(s["ld0"].h, 16)
            e.dma_start(out=gq(CIMo), in_=c_im_g).then_inc(s["ld0"].h, 16)
        P.on("sync", sy_grids)
        grid_ld = s["ld0"].inc(96)

        def sc_g1(e, w=grid_ld):
            e.wait_ge(s["ld0"].h, w)
            e.activation(out=GR[:, DTE:DTE + 8], in_=GR[:, DT:DT + 8], func=AF.Exp)
            e.activation(out=gq(E1), in_=gq(T1o), func=AF.Exp).then_inc(s["sq"].h, 1)
        P.on("scalar", sc_g1)
        g1_tgt = s["sq"].inc()

        def v_g1(e, w=g1_tgt):
            e.wait_ge(s["sq"].h, w)
            VI2(e.tensor_scalar_mul(GR[:, DT2:DT2 + 8], GR[:, DTE:DTE + 8], INV_2PI))
            VW2(e)
            dtb = ap3(GR, DTE, [[1, 8], [0, 32]])
            dt2b = ap3(GR, DT2, [[1, 8], [0, 32]])
            aim3 = ap3(GR, AIMG, [[32, 8], [1, 32]])
            e13 = ap3(GR, E1, [[32, 8], [1, 32]])
            VI2(e.tensor_mul(ap3(GR, T1o, [[32, 8], [1, 32]]), aim3, dtb))
            VI2(e.tensor_mul(ap3(GR, T2o, [[32, 8], [1, 32]]), e13, dtb))
            VI2(e.tensor_mul(ap3(GR, T3o, [[32, 8], [1, 32]]), aim3, dt2b))
            VW2(e)
            VI2(e.tensor_scalar_add(gq(T4o), gq(T3o), 0.25))
            VW2(e)
            VI2(e.tensor_copy(gq(T5o).bitcast(I32), gq(T3o)))
            VI2(e.tensor_copy(gq(T6o).bitcast(I32), gq(T4o)))
            VW2(e)
            VI2(e.tensor_copy(gq(T3o), gq(T5o).bitcast(I32)))
            VI2(e.tensor_copy(gq(T4o), gq(T6o).bitcast(I32)))
            VW2(e)
            VI2(e.scalar_tensor_tensor(out=gq(T5o), in0=gq(T3o), scalar=-TWO_PI,
                                       in1=gq(T1o), op0=ALU.mult, op1=ALU.add))
            VI2(e.scalar_tensor_tensor(out=gq(T6o), in0=gq(T4o), scalar=-TWO_PI,
                                       in1=gq(T1o), op0=ALU.mult, op1=ALU.add))
            VW2(e)
            VI2(e.tensor_scalar_add(gq(T6o), gq(T6o), HALF_PI))
            VW2(e)
            for off in (T5o, T6o):
                VI2(e.tensor_scalar(out=gq(T3o), in0=gq(off), scalar1=PI32,
                                    scalar2=None, op0=ALU.is_gt))
                VI2(e.tensor_scalar(out=gq(T4o), in0=gq(off), scalar1=-PI32,
                                    scalar2=None, op0=ALU.is_lt))
                VW2(e)
                VI2(e.tensor_sub(gq(T4o), gq(T4o), gq(T3o)))
                VW2(e)
                ins = e.scalar_tensor_tensor(out=gq(off), in0=gq(T4o), scalar=TWO_PI,
                                             in1=gq(off), op0=ALU.mult, op1=ALU.add)
                if off == T6o:
                    ins.then_inc(s["stats"].h, 1)
                else:
                    VI2(ins)
                    VW2(e)
        P.on("vector", v_g1)
        vg1_tgt = s["stats"].inc()

        def sc_g2(e, w=vg1_tgt):
            e.wait_ge(s["stats"].h, w)
            e.activation(out=gq(T1o), in_=gq(T2o), func=AF.Exp, scale=-1.0)
            load_act(e, "silu_and_others")
            e.activation(out=gq(T3o), in_=gq(T5o), func=AF.Sin)
            e.activation(out=gq(T4o), in_=gq(T6o), func=AF.Sin).then_inc(s["sq"].h, 1)
        P.on("scalar", sc_g2)
        g2_tgt = s["sq"].inc()

        def v_g2(e, w=g2_tgt):
            e.wait_ge(s["sq"].h, w)
            VI2(e.tensor_mul(gq(LIM), gq(T1o), gq(T3o)))
            VI2(e.tensor_mul(gq(LRE), gq(T1o), gq(T4o)))
            VW2(e)
            VI2(e.tensor_mul(gq(T3o), gq(E1), gq(E1)))
            VI2(e.tensor_mul(gq(T4o), gq(AIMG), gq(AIMG)))
            VW2(e)
            VI2(e.tensor_add(gq(T3o), gq(T3o), gq(T4o)))
            VW2(e)
            VI2(e.reciprocal(gq(T3o), gq(T3o)))
            VI2(e.tensor_scalar_add(gq(T4o), gq(LRE), -1.0))
            VW2(e)
            VI2(e.tensor_mul(gq(T5o), gq(T4o), gq(E1)))
            VI2(e.tensor_mul(gq(T6o), gq(LIM), gq(AIMG)))
            VW2(e)
            VI2(e.tensor_sub(gq(T5o), gq(T6o), gq(T5o)))
            VW2(e)
            VI2(e.tensor_mul(gq(T5o), gq(T5o), gq(T3o)))
            VI2(e.tensor_mul(gq(T6o), gq(LIM), gq(E1)))
            VI2(e.tensor_mul(gq(T1o), gq(T4o), gq(AIMG)))
            VW2(e)
            VI2(e.tensor_add(gq(T6o), gq(T6o), gq(T1o)))
            VW2(e)
            VI2(e.tensor_mul(gq(T6o), gq(T6o), gq(T3o)))
            VW2(e)
            VI2(e.tensor_scalar_mul(gq(T6o), gq(T6o), -1.0))
            VW2(e)
            VI2(e.tensor_mul(gq(T1o), gq(CREo), gq(T5o)))
            VI2(e.tensor_mul(gq(T2o), gq(CIMo), gq(T6o)))
            VI2(e.tensor_mul(gq(T3o), gq(CREo), gq(T6o)))
            VI2(e.tensor_mul(gq(T4o), gq(CIMo), gq(T5o)))
            VW2(e)
            VI2(e.tensor_sub(gq(T1o), gq(T1o), gq(T2o)))
            VI2(e.tensor_add(gq(T3o), gq(T3o), gq(T4o)))
            VW2(e)
            VI2(e.tensor_copy(gq(CREo), gq(T1o)))
            VI2(e.tensor_copy(gq(CIMo), gq(T3o)))
            VW2(e)
            VI2(e.tensor_mul(gq(T2o), gq(CREo), gq(LRE)))
            VI2(e.tensor_mul(gq(T4o), gq(CIMo), gq(LIM)))
            VI2(e.tensor_mul(gq(T5o), gq(CREo), gq(LIM)))
            VI2(e.tensor_mul(gq(T6o), gq(CIMo), gq(LRE)))
            VW2(e)
            VI2(e.tensor_sub(gq(T2o), gq(T2o), gq(T4o)))
            VI2(e.tensor_add(gq(T5o), gq(T5o), gq(T6o)))
            VW2(e)
            VI2(e.tensor_scalar_mul(gs(CT2RE), gq(T2o), 2.0))
            VI2(e.tensor_scalar_mul(gs(CT2IM), gq(T5o), 2.0))
            VI2(e.tensor_scalar_mul(gq(CREo), gq(CREo), 2.0))
            e.tensor_scalar_mul(gq(CIMo), gq(CIMo), -2.0).then_inc(s["stats"].h, 1)
        P.on("vector", v_g2)
        grids_done = s["stats"].inc()

        kt_store_tgt = 0
        bank_last_evac = {}
        for g in range(NG if TCUT >= 2 else 0):
            ph_tgts = []
            for hf in range(2):
                Pb = PH[hf]
                def v_pb(e, g=g, hf=hf, Pb=Pb, w=grids_done, w2=w2q_done[7]):
                    if g == 0 and hf == 0:
                        e.wait_ge(s["stats"].h, w)
                        e.wait_ge(s["ld0"].h, w2)
                    e.memset(ap3(Pb, 0, [[256, 16], [1, 1]]), 1.0)
                    e.memset(ap3(Pb, 128, [[256, 16], [1, 1]]), 0.0)
                    co = g * 32 + 16 * hf
                    VI2(e.tensor_copy(GR[:, MRE:MRE + 16], GR[:, LRE + co:LRE + co + 16]))
                    VI2(e.tensor_copy(GR[:, MIM:MIM + 16], GR[:, LIM + co:LIM + co + 16]))
                    VW2(e)
                    for k in range(7):
                        cnt = 1 << k
                        src_re = ap3(Pb, 0, [[256, 16], [1, cnt]])
                        src_im = ap3(Pb, 128, [[256, 16], [1, cnt]])
                        dst_re = ap3(Pb, cnt, [[256, 16], [1, cnt]])
                        dst_im = ap3(Pb, 128 + cnt, [[256, 16], [1, cnt]])
                        mre = ap3(GR, MRE, [[1, 16], [0, cnt]])
                        mim = ap3(GR, MIM, [[1, 16], [0, cnt]])
                        t1 = ap3(h2f, kb1o, [[cnt, 16], [1, cnt]])
                        t2 = ap3(h2f, kb2o, [[cnt, 16], [1, cnt]])
                        VI2(e.tensor_mul(t1, src_re, mim))
                        VI2(e.tensor_mul(t2, src_im, mre))
                        VW2(e)
                        VI2(e.tensor_add(dst_im, t1, t2))
                        VI2(e.tensor_mul(t1, src_re, mre))
                        VI2(e.tensor_mul(t2, src_im, mim))
                        VW2(e)
                        VI2(e.tensor_sub(dst_re, t1, t2))
                        VW2(e)
                        VI2(e.tensor_mul(GR[:, TAa:TAa + 16], GR[:, MRE:MRE + 16], GR[:, MIM:MIM + 16]))
                        VI2(e.tensor_mul(GR[:, TBb:TBb + 16], GR[:, MRE:MRE + 16], GR[:, MRE:MRE + 16]))
                        VI2(e.tensor_mul(GR[:, TCc:TCc + 16], GR[:, MIM:MIM + 16], GR[:, MIM:MIM + 16]))
                        VW2(e)
                        VI2(e.tensor_sub(GR[:, MRE:MRE + 16], GR[:, TBb:TBb + 16], GR[:, TCc:TCc + 16]))
                        VI2(e.tensor_scalar_mul(GR[:, MIM:MIM + 16], GR[:, TAa:TAa + 16], 2.0))
                        VW2(e)
                    VI2(e.tensor_copy(GR[:, LQRE + co:LQRE + co + 16], GR[:, MRE:MRE + 16]))
                    e.tensor_copy(GR[:, LQIM + co:LQIM + co + 16],
                                  GR[:, MIM:MIM + 16]).then_inc(s["w1"].h, 1)
                P.on("vector", v_pb)
                ph_tgts.append(s["w1"].inc())

            for hf in range(2):
                Pb = PH[hf]
                ph_tgt = ph_tgts[hf]
                for rnd in range(8 if GCUT >= 2 else 0):
                    bank = pF if rnd % 2 == 0 else pG
                    bkey = "pF" if rnd % 2 == 0 else "pG"
                    bw_prev = bank_last_evac.get(bkey, 0)
                    def t_wgt(e, Pb=Pb, rnd=rnd, bank=bank, w=ph_tgt, bp=bw_prev):
                        if bp > 0:
                            e.wait_ge(s["ztfree"].h, bp)
                        e.wait_ge(s["w1"].h, w)
                        for j in range(4):
                            pl = rnd * 4 + j
                            ins = e.transpose(bank[:, j * 128:(j + 1) * 128],
                                              Pb[:, pl * 128:(pl + 1) * 128],
                                              ident32[:])
                        ins.then_inc(s["zt"].h, 1)
                    P.on("tensor", t_wgt)
                    zt_tgt = s["zt"].inc()

                    def v_wge(e, hf=hf, rnd=rnd, bank=bank, w=zt_tgt):
                        e.wait_ge(s["zt"].h, w)
                        for j in range(4):
                            pl = rnd * 4 + j
                            nl, r = pl // 2, pl % 2
                            nr = r * 32 + (16 * hf + nl)
                            out_ap = ap3(wgstage, nr * 64, [[4096, 2], [1, 64]])
                            in_ap = ap3(bank[:], j * 128, [[64, 2], [1, 64]])
                            ins = e.tensor_copy(out_ap, in_ap)
                        ins.then_inc(s["ztfree"].h, 1)
                    P.on("vector", v_wge)
                    bank_last_evac[bkey] = s["ztfree"].inc()

            wg_ev_done = s["ztfree"].v

            def v_k(e, g=g, w=wg_ev_done, skip=(GCUT < 4)):
                if skip:
                    return
                e.wait_ge(s["ztfree"].h, w)
                kb1a = ap3(h2f, kb1o, [[128, 16], [1, 128]])
                kb2a = ap3(h2f, kb2o, [[128, 16], [1, 128]])
                first = True
                for hf in range(2):
                    Pb = PH[hf]
                    co = g * 32 + 16 * hf
                    pre = ap3(Pb, 0, [[256, 16], [1, 128]])
                    pim = ap3(Pb, 128, [[256, 16], [1, 128]])
                    creb = ap3(GR, CREo + co, [[1, 16], [0, 128]])
                    cimb = ap3(GR, CIMo + co, [[1, 16], [0, 128]])
                    for src, cb in ((pre, creb), (pim, cimb)):
                        if first:
                            VI2(e.tensor_mul(kb1a, src, cb))
                            first = False
                        else:
                            VI2(e.tensor_mul(kb2a, src, cb))
                            VW2(e)
                            VI2(e.tensor_add(kb1a, kb1a, kb2a))
                        VW2(e)
                for width in (1024, 512, 256, 128):
                    VI2(e.tensor_add(h2f[:, kb1o:kb1o + width],
                                     h2f[:, kb1o:kb1o + width],
                                     h2f[:, kb1o + width:kb1o + 2 * width]))
                    VW2(e)
                VI2(e.tensor_add(h2f[:, kb1o:kb1o + 1], h2f[:, kb1o:kb1o + 1],
                                 GR[:, DSK + g:DSK + g + 1]))
                e.memset(kpad[:, 0:128], 0.0)
                VW2(e)
                e.tensor_copy(kpad[:, 128:256],
                              h2f[:, kb1o:kb1o + 128]).then_inc(s["bw"].h, 1)
            P.on("vector", v_k)
            k_tgt = s["bw"].inc() if GCUT >= 4 else s["bw"].v

            def t_kt(e, w=k_tgt, skip=(GCUT < 4)):
                if skip:
                    return
                e.wait_ge(s["bw"].h, w)
                e.transpose(pE[:, 0:128], kpad[:, 0:128], ident[:])
                e.transpose(pE[:, 128:256], kpad[:, 128:256],
                            ident[:]).then_inc(s["zt"].h, 1)
            P.on("tensor", t_kt)
            kt_tgt = s["zt"].inc() if GCUT >= 4 else s["zt"].v

            def v_kte(e, w=kt_tgt, wst=kt_store_tgt, skip=(GCUT < 4)):
                if skip:
                    return
                e.wait_ge(s["zt"].h, w)
                if wst > 0:
                    e.wait_ge(s["ld0"].h, wst)
                e.tensor_copy(ktst[:, 0:128], pE[:, 0:128])
                e.tensor_copy(ktst[:, 128:256], pE[:, 128:256]).then_inc(s["ztfree"].h, 1)
            P.on("vector", v_kte)
            kte_tgt = s["ztfree"].inc() if GCUT >= 4 else s["ztfree"].v

            def sy_kt(e, g=g, w=kte_tgt, skip=(GCUT < 4)):
                if skip:
                    return
                e.wait_ge(s["ztfree"].h, w)
                for tile in range(2):
                    for qq in range(4):
                        e.dma_start(out=kT_hbm[4 * g + qq, tile * 128:(tile + 1) * 128, :],
                                    in_=ktst[:, tile * 128 + qq * 32: tile * 128 + qq * 32 + 32]
                                    ).then_inc(s["ld0"].h, 16)
            P.on("sync", sy_kt)
            if GCUT >= 4:
                kt_store_tgt = s["ld0"].inc(128)

            if GCUT >= 3:
                def sy_wg(e, g=g, w=wg_ev_done):
                    e.wait_ge(s["ztfree"].h, w)
                    e.dma_start(out=wg_hbm[g], in_=wgstage).then_inc(s["ld0"].h, 16)
                P.on("sync", sy_wg)
                wg_st = s["ld0"].inc(16)
                P.on("vector", lambda e, w=wg_st: e.wait_ge(s["ld0"].h, w))

            grids4 = [(LRE, LTRE, GR), (LIM, LTIM, GR),
                      (CT2RE, CTTRE, scl), (CT2IM, CTTIM, scl),
                      (LQRE, LQTRE, GR), (LQIM, LQTIM, GR)]
            ltstage = h2f[:, kb1o:kb1o + 128]
            for srco, dsto, srcb in (grids4 if GCUT >= 5 else []):
                def v_lts(e, g=g, srco=srco, srcb=srcb):
                    src = bass.AP(srcb.tensor, srcb.offset + srco + g * 32,
                                  [[srcb.ap[0][0], 128], [0, 4], [1, 32]])
                    e.tensor_copy(ap3(h2f, kb1o, [[32, 4], [1, 32]]),
                                  src).then_inc(s["bw"].h, 1)
                P.on("vector", v_lts)
                lts_tgt = s["bw"].inc()

                def t_lt(e, w=lts_tgt, fr=s["ztfree"].v):
                    if fr > 0:
                        e.wait_ge(s["ztfree"].h, fr)
                    e.wait_ge(s["bw"].h, w)
                    e.transpose(pG[:, 0:128], ltstage,
                                ident32[:]).then_inc(s["zt"].h, 1)
                P.on("tensor", t_lt)
                lt_tgt = s["zt"].inc()

                def v_lte(e, g=g, dsto=dsto, w=lt_tgt):
                    e.wait_ge(s["zt"].h, w)
                    pga = pG[:]
                    for m in range(4):
                        src = bass.AP(pga.tensor, pga.offset + 32 * m * 512 + m,
                                      [[512, 32], [4, 32]])
                        ins = e.tensor_copy(
                            scl[32 * m:32 * m + 32, dsto + g * 32: dsto + g * 32 + 32], src)
                    ins.then_inc(s["ztfree"].h, 1)
                P.on("vector", v_lte)
                s["ztfree"].inc()

        def sy_lq(e, w=s["w1"].v, skip=(TCUT < 2)):
            if skip:
                return
            e.wait_ge(s["w1"].h, w)
            e.dma_start(out=lamq_hbm, in_=GR[:, LQRE:LQRE + 512]).then_inc(s["ld0"].h, 16)
        P.on("sync", sy_lq)
        if TCUT >= 2:
            s["ld0"].inc(16)

        for b in range(8 if TCUT >= 3 else 0):
            def v_wc(e, b=b, w=s["ztfree"].v):
                if b == 0:
                    e.wait_ge(s["ztfree"].h, w)
                e.memset(ap3(PH[0], 0, [[128, 32], [1, 1]]), 1.0)
                e.memset(ap3(PH[1], 0, [[128, 32], [1, 1]]), 0.0)
                VI2(e.tensor_copy(scl[:, MTRE:MTRE + 32],
                                  scl[:, LTRE + 32 * b: LTRE + 32 * b + 32]))
                VI2(e.tensor_copy(scl[:, MTIM:MTIM + 32],
                                  scl[:, LTIM + 32 * b: LTIM + 32 * b + 32]))
                VW2(e)
                for k in range(7):
                    cnt = 1 << k
                    sre = ap3(PH[0], 0, [[128, 32], [1, cnt]])
                    sim = ap3(PH[1], 0, [[128, 32], [1, cnt]])
                    dre = ap3(PH[0], cnt, [[128, 32], [1, cnt]])
                    dim = ap3(PH[1], cnt, [[128, 32], [1, cnt]])
                    mre = ap3(scl, MTRE, [[1, 32], [0, cnt]])
                    mim = ap3(scl, MTIM, [[1, 32], [0, cnt]])
                    t1 = ap3(h2f, kb1o, [[cnt, 32], [1, cnt]])
                    t2 = ap3(h2f, kb2o, [[cnt, 32], [1, cnt]])
                    VI2(e.tensor_mul(t1, sre, mim))
                    VI2(e.tensor_mul(t2, sim, mre))
                    VW2(e)
                    VI2(e.tensor_add(dim, t1, t2))
                    VI2(e.tensor_mul(t1, sre, mre))
                    VI2(e.tensor_mul(t2, sim, mim))
                    VW2(e)
                    if k == 6:
                        e.tensor_sub(dre, t1, t2).then_inc(s["bw"].h, 1)
                    else:
                        VI2(e.tensor_sub(dre, t1, t2))
                        VW2(e)
                        VI2(e.tensor_mul(scl[:, MTA:MTA + 32], scl[:, MTRE:MTRE + 32],
                                         scl[:, MTIM:MTIM + 32]))
                        VI2(e.tensor_mul(scl[:, MTB:MTB + 32], scl[:, MTRE:MTRE + 32],
                                         scl[:, MTRE:MTRE + 32]))
                        VI2(e.tensor_mul(scl[:, MTC:MTC + 32], scl[:, MTIM:MTIM + 32],
                                         scl[:, MTIM:MTIM + 32]))
                        VW2(e)
                        VI2(e.tensor_sub(scl[:, MTRE:MTRE + 32], scl[:, MTB:MTB + 32],
                                         scl[:, MTC:MTC + 32]))
                        VI2(e.tensor_scalar_mul(scl[:, MTIM:MTIM + 32],
                                                scl[:, MTA:MTA + 32], 2.0))
                        VW2(e)
            P.on("vector", v_wc)
            pp_tgt = s["bw"].inc()

            for comp in range(2):
                stc = s["ld0"].v
                def v_wcs(e, b=b, comp=comp, w=pp_tgt, ws=stc):
                    e.wait_ge(s["bw"].h, w)
                    if ws > 0:
                        e.wait_ge(s["ld0"].h, ws)
                    for phh in range(2):
                        po = phh * 64
                        ppre_s = ap3(PH[0], po, [[128, 32], [1, 64]])
                        ppim_s = ap3(PH[1], po, [[128, 32], [1, 64]])
                        ctre_b = ap3(scl, CTTRE + 32 * b, [[1, 32], [0, 64]])
                        ctim_b = ap3(scl, CTTIM + 32 * b, [[1, 32], [0, 64]])
                        t1 = ap3(h2f, kb1o, [[64, 32], [1, 64]])
                        t2 = ap3(h2f, kb2o, [[64, 32], [1, 64]])
                        out = ap3(wcstage, po, [[128, 32], [1, 64]])
                        last = (phh == 1)
                        if comp == 0:
                            VI2(e.tensor_mul(t1, ppre_s, ctre_b))
                            VI2(e.tensor_mul(t2, ppim_s, ctim_b))
                            VW2(e)
                            if last:
                                e.tensor_sub(out, t1, t2).then_inc(s["bwc"].h, 1)
                            else:
                                VI2(e.tensor_sub(out, t1, t2))
                        else:
                            VI2(e.tensor_mul(t1, ppim_s, ctre_b))
                            VI2(e.tensor_mul(t2, ppre_s, ctim_b))
                            VW2(e)
                            VI2(e.tensor_add(t1, t1, t2))
                            VW2(e)
                            if last:
                                e.tensor_scalar_mul(out, t1, -1.0).then_inc(s["bwc"].h, 1)
                            else:
                                VI2(e.tensor_scalar_mul(out, t1, -1.0))
                P.on("vector", v_wcs)
                wcs_tgt = s["bwc"].inc()

                def sy_wcs(e, b=b, comp=comp, w=wcs_tgt):
                    e.wait_ge(s["bwc"].h, w)
                    dst = wcre_hbm if comp == 0 else wcim_hbm
                    e.dma_start(out=dst[:, b * 4096:(b + 1) * 4096],
                                in_=wcstage).then_inc(s["ld0"].h, 16)
                P.on("sync", sy_wcs)
                s["ld0"].inc(16)

        tables_done = ("ld0", s["ld0"].v)
    else:
        tables_done = ("deq", 16)

    def emit_ffn(x_src, x_dst, w1g_gate):
        gate_sem, gate_val = w1g_gate
        SU, SQ, M, MSQ, M2, VAR, VEPS, RCP, VR, LNV, LNW, R1, R2, SS, CC, S2, C2, TT = \
            (0, 4, 8, 12, 16, 20, 24, 28, 32, 36, 40, 44, 48, 52, 56, 60, 64, 68)
        g4 = lambda off: statrow[:, off:off + 4]
        g1 = lambda off, i: statrow[:, off + i:off + i + 1]

        def VI(ins):
            s["vch"].inc()
            ins.then_inc(s["vch"].h, 1)

        def VW(e):
            e.wait_ge(s["vch"].h, s["vch"].v)

        for ch in range(8):
            xchunk = wtmp
            xfree_need = s["xfree"].v

            def sy_x(e, ch=ch, wf=xfree_need, first=(ch == 0),
                     gs=gate_sem, gv=gate_val):
                if first:
                    e.wait_ge(s[gs].h, gv)
                if wf > 0:
                    e.wait_ge(s["xfree"].h, wf)
                for lt4 in range(4):
                    lt = ch * 4 + lt4
                    e.dma_start(out=xchunk[:, lt4 * D:(lt4 + 1) * D],
                                in_=x_src[lt * 128:(lt + 1) * 128, :]).then_inc(s["xld"].h, 16)
            P.on("sync", sy_x)
            xld_tgt = s["xld"].inc(64)

            xst_prev = s["xst"].v
            def sc_sq(e, w=xld_tgt, bw=bias_ready, wx=xst_prev, ch=ch,
                      gs=gate_sem, gv=gate_val):
                if ch == 0:
                    e.wait_ge(s[gs].h, gv)
                load_act(e, "natural_log_exp_and_others")
                e.wait_ge(s["stats"].h, bw)
                e.wait_ge(s["xld"].h, w)
                if wx > 0:
                    e.wait_ge(s["xst"].h, wx)
                for lt4 in range(4):
                    ins = e.activation(out=xsbuf[:], in_=xchunk[:, lt4 * D:(lt4 + 1) * D],
                                       func=AF.Square, accum_out=g1(SQ, lt4))
                ins.then_inc(s["sq"].h, 1)
            P.on("scalar", sc_sq)
            sq_tgt = s["sq"].inc()

            def v_chain(e, w=xld_tgt, wsq=sq_tgt):
                e.wait_ge(s["xld"].h, w)
                for lt4 in range(4):
                    ins = e.reduce_sum(g1(SU, lt4), xchunk[:, lt4 * D:(lt4 + 1) * D], axis=AX.X)
                VI(ins)
                e.wait_ge(s["sq"].h, wsq)
                VW(e)
                VI(e.tensor_scalar_mul(g4(M), g4(SU), 1.0 / D))
                VI(e.tensor_scalar_mul(g4(MSQ), g4(SQ), 1.0 / D))
                VW(e)
                VI(e.tensor_mul(g4(M2), g4(M), g4(M)))
                VW(e)
                VI(e.tensor_sub(g4(VAR), g4(MSQ), g4(M2)))
                VW(e)
                VI(e.tensor_scalar_add(g4(VEPS), g4(VAR), EPS))
                VW(e)
                VI(e.reciprocal(g4(RCP), g4(VEPS)))
                VW(e)
                VI(e.tensor_mul(g4(VR), g4(VAR), g4(RCP)))
                VW(e)
                e.tensor_scalar_add(g4(VR), g4(VR), EPS).then_inc(s["stats"].h, 1)
            P.on("vector", v_chain)
            st_tgt = s["stats"].inc()

            ln_tgt = s["sq"].inc()
            rsq_tgt = s["sq"].inc()

            def sc_rsq(e, w=st_tgt, wl=ln_tgt):
                e.wait_ge(s["stats"].h, w)
                e.activation(out=g4(LNV), in_=g4(VEPS), func=AF.Ln)
                e.activation(out=g4(LNW), in_=g4(VR), func=AF.Ln).then_inc(s["sq"].h, 1)
                e.wait_ge(s["sq"].h, wl)
                e.activation(out=g4(R1), in_=g4(LNV), func=AF.Exp, scale=-0.5)
                e.activation(out=g4(R2), in_=g4(LNW), func=AF.Exp,
                             scale=-0.5).then_inc(s["sq"].h, 1)
            P.on("scalar", sc_rsq)

            def v_chain2(e, w=rsq_tgt):
                e.wait_ge(s["sq"].h, w)
                VI(e.tensor_mul(g4(SS), g4(R1), g4(R2)))
                e.tensor_scalar(out=g4(S2), in0=g4(R1), scalar1=0.5, scalar2=1.0,
                                op0=ALU.mult, op1=ALU.add)
                VI(e.tensor_mul(g4(TT), g4(M), g4(R1)))
                VW(e)
                e.scalar_tensor_tensor(out=g4(CC), in0=g4(M), scalar=-1.0,
                                       in1=g4(SS), op0=ALU.mult, op1=ALU.mult)
                e.tensor_scalar_mul(g4(C2), g4(TT), -0.5).then_inc(s["stats"].h, 1)
            P.on("vector", v_chain2)
            st_tgt = s["stats"].inc()

            for lt4 in range(4):
                xst_prev2 = s["xst"].v

                def sc_apply(e, lt4=lt4, w=st_tgt, wx=xst_prev2):
                    e.wait_ge(s["stats"].h, w)
                    if wx > 0:
                        e.wait_ge(s["xst"].h, wx)
                    e.activation(out=xsbuf[:], in_=xchunk[:, lt4 * D:(lt4 + 1) * D],
                                 func=AF.Identity,
                                 scale=g1(SS, lt4), bias=g1(CC, lt4)).then_inc(s["xs"].h, 1)
                    e.activation(out=xhalf[:, lt4 * D:(lt4 + 1) * D],
                                 in_=xchunk[:, lt4 * D:(lt4 + 1) * D],
                                 func=AF.Identity,
                                 scale=g1(S2, lt4), bias=g1(C2, lt4)).then_inc(s["xfree"].h, 1)
                P.on("scalar", sc_apply)
                xs_tgt = s["xs"].inc()
                s["xfree"].inc()

                def t_xst(e, w=xs_tgt, fr=s["xsfm"].v,
                          first=(ch == 0 and lt4 == 0)):
                    if first:
                        e.wait_ge(s["gld"].h, ident_ready)
                    e.wait_ge(s["xs"].h, w)
                    if fr > 0:
                        e.wait_ge(s["xsfm"].h, fr)
                    for kt in range(8):
                        ins = e.transpose(pE[:, kt * 128:(kt + 1) * 128],
                                          xsbuf[:, kt * 128:(kt + 1) * 128], ident[:])
                    ins.then_inc(s["xst"].h, 1)
                P.on("tensor", t_xst)
                xst_tgt = s["xst"].inc()

                def v_xsfm(e, w=xst_tgt, lt4=lt4):
                    e.wait_ge(s["xst"].h, w)
                    for kt in range(8):
                        ins = e.tensor_copy(
                            xs_fm[:, kt * 512 + lt4 * 128: kt * 512 + (lt4 + 1) * 128],
                            pE[:, kt * 128:(kt + 1) * 128])
                    ins.then_inc(s["xsfm"].h, 1)
                P.on("vector", v_xsfm)
                s["xsfm"].inc()

            xsfm_need = s["xsfm"].v
            for ft in range(32):
                bank = pA if ft % 2 == 0 else pB
                h2_before = s["h2"].v
                def t_fc1(e, ft=ft, bank=bank, w=xsfm_need, hn=h2_before,
                          ch=ch, gs=gate_sem, gv=gate_val):
                    if ft == 0:
                        if ch == 0:
                            e.wait_ge(s[gs].h, gv)
                        e.wait_ge(s["xsfm"].h, w)
                    if ft >= 2:
                        e.wait_ge(s["h2"].h, hn - 1)
                    for kt in range(8):
                        col = kt * FF + ft * 128
                        ins = e.matmul(bank[:], w1g[:, col:col + 128],
                                       xs_fm[:, kt * 512:(kt + 1) * 512],
                                       start=(kt == 0), stop=(kt == 7))
                    ins.then_inc(s["fc1"].h, 1)
                P.on("tensor", t_fc1)
                fc1_tgt = s["fc1"].inc()

                def sc_silu(e, ft=ft, bank=bank, w=fc1_tgt, ch=ch,
                            wq=w2q_done[7] if w2q_done[7] else 0):
                    if ft == 0:
                        load_act(e, "silu_and_others")
                    if ft == 0 and ch == 0:
                        if wq:
                            e.wait_ge(s["ld0"].h, wq)
                    e.wait_ge(s["fc1"].h, w)
                    e.activation(out=h2_fm[:, ft * 512:(ft + 1) * 512], in_=bank[:],
                                 func=AF.Silu,
                                 bias=b1p_col[:, ft:ft + 1]).then_inc(s["h2"].h, 1)
                P.on("scalar", sc_silu)
                s["h2"].inc()

            h2_need = s["h2"].v
            for dt_ in range(8):
                wb = w2buf[:, (dt_ % 2) * 4096:(dt_ % 2 + 1) * 4096]
                w2f_before = s["w2free"].v
                if w2f_before >= 1:
                    P.on("sync", lambda e, w=w2f_before - 1: (
                        e.wait_ge(s["w2free"].h, w) if w > 0 else None))
                def sy_w2(e, dt_=dt_, wb=wb, ch=ch, wd=w2q_done[dt_]):
                    if ch == 0:
                        e.wait_ge(s["ld0"].h, wd)
                    e.dma_start(out=wb, in_=w2q_hbm[dt_]).then_inc(s["w2ld"].h, 16)
                P.on("sync", sy_w2)
                w2_tgt = s["w2ld"].inc(16)

                bank = pC if dt_ % 2 == 0 else pD
                zev_before = s["zev"].v
                def t_fc2(e, wb=wb, bank=bank, w=w2_tgt, hn=h2_need, dt_=dt_, zn=zev_before):
                    e.wait_ge(s["w2ld"].h, w)
                    if dt_ == 0:
                        e.wait_ge(s["h2"].h, hn)
                    if dt_ >= 2:
                        e.wait_ge(s["zev"].h, zn - 1)
                    for ft in range(32):
                        ins = e.matmul(bank[:], wb[:, ft * 128:(ft + 1) * 128],
                                       h2_fm[:, ft * 512:(ft + 1) * 512],
                                       start=(ft == 0), stop=(ft == 31))
                    ins.then_inc(s["fc2"].h, 1)
                P.on("tensor", t_fc2)
                fc2_tgt = s["fc2"].inc()

                def sc_zev(e, dt_=dt_, bank=bank, w=fc2_tgt):
                    e.wait_ge(s["fc2"].h, w)
                    e.activation(out=zfm[:], in_=bank[:], func=AF.Identity,
                                 bias=b2q_col[:, dt_:dt_ + 1]).then_inc(s["zev"].h, 1)
                    e.activation(out=statrow[:, 23:24], in_=statrow[:, 2:3],
                                 func=AF.Identity).then_inc(s["w2free"].h, 1)
                P.on("scalar", sc_zev)
                zev_tgt = s["zev"].inc()
                s["w2free"].inc()

                ztf_before = s["ztfree"].v
                def t_zt(e, w=zev_tgt, fr=ztf_before):
                    e.wait_ge(s["zev"].h, w)
                    if fr > 0:
                        e.wait_ge(s["ztfree"].h, fr)
                    for lt4 in range(4):
                        ins = e.transpose(pF[:, lt4 * 128:(lt4 + 1) * 128],
                                          zfm[:, lt4 * 128:(lt4 + 1) * 128], ident32[:])
                    ins.then_inc(s["zt"].h, 1)
                P.on("tensor", t_zt)
                zt_tgt = s["zt"].inc()

                def v_x1(e, dt_=dt_, w=zt_tgt):
                    e.wait_ge(s["zt"].h, w)
                    for lt4 in range(4):
                        ins = e.tensor_add(
                            x1part[lt4][:, dt_ * 128:(dt_ + 1) * 128],
                            pF[:, lt4 * 128:(lt4 + 1) * 128],
                            xhalf[:, lt4 * D + dt_ * 128: lt4 * D + (dt_ + 1) * 128])
                    ins.then_inc(s["ztfree"].h, 1)
                    if dt_ == 7:
                        e.tensor_copy(statrow[:, 22:23],
                                      statrow[:, 2:3]).then_inc(s["x1rdy"].h, 1)
                P.on("vector", v_x1)
                s["ztfree"].inc()
            x1r_tgt = s["x1rdy"].inc()

            def sy_x1(e, ch=ch, w=x1r_tgt):
                e.wait_ge(s["x1rdy"].h, w)
                for lt4 in range(4):
                    lt = ch * 4 + lt4
                    e.dma_start(out=x_dst[lt * 128:(lt + 1) * 128, :],
                                in_=x1part[lt4][:]).then_inc(s["x1st"].h, 16)
            P.on("sync", sy_x1)
            x1st_tgt = s["x1st"].inc(64)
            P.on("scalar", lambda e, w=x1st_tgt: e.wait_ge(s["x1st"].h, w))
            P.on("vector", lambda e, w=x1st_tgt: e.wait_ge(s["x1st"].h, w))

    emit_ffn(x_in, x1_hbm, tables_done)

    if DEV_STAGE >= 2:
        barrier(pre_waits=[("sync", "x1st", s["x1st"].v),
                           ("gpsimd", "gdma", s["gdma"].v),
                           ("tensor", "ztfree", s["ztfree"].v)])

        P2CUT = int(os.environ.get("MK_P2CUT", "3"))
        u_T = w1g
        in_WT = h2_fm[:, 8192:16384]
        in_bB_sb = wtmp[:, 0:1024]
        x1chunk = qbuf
        t2fm = [xs_fm[:, 0:1024], xs_fm[:, 1024:2048]]
        winw = wtmp[:, 1024:2048]
        winwb = zfm[:].bitcast(BF16)

        def sy_inb(e):
            e.dma_start(out=in_bB_sb, in_=in_bB).then_inc(s["ld0"].h, 16)
        P.on("sync", sy_inb)
        inb_tgt = s["ld0"].inc(16)

        for dt_ in range(8 if P2CUT >= 1 else 0):
            def sy_winw(e, dt_=dt_, wprev=s["uev"].v):
                if dt_ > 0:
                    e.wait_ge(s["uev"].h, wprev)
                e.dma_start(out=winw, in_=in_W[dt_ * 128:(dt_ + 1) * 128, :]).then_inc(s["ld0"].h, 16)
            P.on("sync", sy_winw)
            w_tgt = s["ld0"].inc(16)

            def v_cast(e, w=w_tgt, wt=s["xst"].v):
                e.wait_ge(s["ld0"].h, w)
                if wt > 0:
                    e.wait_ge(s["xst"].h, wt)
                e.tensor_copy(winwb, winw).then_inc(s["uev"].h, 1)
            P.on("vector", v_cast)
            cast_tgt = s["uev"].inc()

            def t_wtr(e, dt_=dt_, w=cast_tgt, fr=s["xsfm"].v):
                e.wait_ge(s["uev"].h, w)
                if fr > 0:
                    e.wait_ge(s["xsfm"].h, fr)
                for kt in range(8):
                    ins = e.transpose(pE[:, kt * 128:(kt + 1) * 128],
                                      winwb[:, kt * 128:(kt + 1) * 128], ident[:])
                ins.then_inc(s["xst"].h, 1)
            P.on("tensor", t_wtr)
            wtr_tgt = s["xst"].inc()

            def v_wev(e, dt_=dt_, w=wtr_tgt):
                e.wait_ge(s["xst"].h, w)
                for kt in range(8):
                    ins = e.tensor_copy(
                        in_WT[:, kt * 1024 + dt_ * 128: kt * 1024 + (dt_ + 1) * 128],
                        pE[:, kt * 128:(kt + 1) * 128])
                ins.then_inc(s["xsfm"].h, 1)
            P.on("vector", v_wev)
            s["xsfm"].inc()
        inwt_done = s["xsfm"].v

        wlstage = wtmp[:, 2048:3072]
        wstg2b = sclbuf[:].bitcast(BF16)[:, 5248:6272]
        wlist = [(glu_W, gluWT_hbm, 16), (out_W, outWT_hbm, 8)]
        wl_prev_store = 0
        wl_prev_tr = {0: 0, 1: 0}
        for W_in, W_out, n_ot in wlist:
            for ot in range(n_ot):
                def sy_wl(e, W_in=W_in, ot=ot, w=s["bwc"].v, wt=wl_prev_tr[0],
                          wt2=wl_prev_tr[1]):
                    e.wait_ge(s["bwc"].h, w)
                    if wt2 > 0:
                        e.wait_ge(s["zt"].h, wt2)
                    e.dma_start(out=wlstage,
                                in_=W_in[ot * 128:(ot + 1) * 128, :]).then_inc(s["w1"].h, 16)
                P.on("sync", sy_wl)
                wl_tgt = s["w1"].inc(16)

                ztr = []
                for bi, bank in enumerate((pF, pG)):
                    def t_wtr(e, bi=bi, bank=bank, w=wl_tgt, fr=wl_prev_store):
                        e.wait_ge(s["w1"].h, w)
                        if fr > 0:
                            e.wait_ge(s["ztfree"].h, fr)
                        for kt in range(4):
                            ins = e.transpose(bank[:, kt * 128:(kt + 1) * 128],
                                              wlstage[:, (bi * 4 + kt) * 128:
                                                      (bi * 4 + kt + 1) * 128],
                                              ident32[:])
                        ins.then_inc(s["zt"].h, 1)
                    P.on("tensor", t_wtr)
                    ztr.append(s["zt"].inc())
                wl_prev_tr = {0: ztr[0], 1: ztr[1]}

                def v_wev(e, w=ztr, ws=wl_prev_store):
                    if ws > 0:
                        e.wait_ge(s["ld0"].h, 0)
                    e.wait_ge(s["zt"].h, w[0])
                    e.tensor_copy(wstg2b[:, 0:512], pF[:])
                    e.wait_ge(s["zt"].h, w[1])
                    e.tensor_copy(wstg2b[:, 512:1024], pG[:]).then_inc(s["ztfree"].h, 1)
                P.on("vector", v_wev)
                we_tgt = s["ztfree"].inc()

                def sy_wst(e, W_out=W_out, ot=ot, w=we_tgt):
                    e.wait_ge(s["ztfree"].h, w)
                    e.dma_start(out=W_out[:, ot * 1024:(ot + 1) * 1024],
                                in_=wstg2b).then_inc(s["ld0"].h, 16)
                P.on("sync", sy_wst)
                wl_store = s["ld0"].inc(16)
                P.on("vector", lambda e, w=wl_store: e.wait_ge(s["ld0"].h, w))


        SU, SQ, M, MSQ, M2, VAR, VEPS, LNV, RR = (0, 4, 8, 12, 16, 20, 24, 36, 44)
        g4 = lambda off: statrow[:, off:off + 4]
        g1 = lambda off, i: statrow[:, off + i:off + i + 1]

        def VI(ins):
            s["vch"].inc()
            ins.then_inc(s["vch"].h, 1)

        def VW(e):
            e.wait_ge(s["vch"].h, s["vch"].v)

        for ch in range(8 if P2CUT >= 2 else 0):
            xfree_need = s["xfree"].v
            def sy_x1(e, ch=ch, wf=xfree_need):
                if wf > 0:
                    e.wait_ge(s["xfree"].h, wf)
                for lt4 in range(4):
                    lt = ch * 4 + lt4
                    e.dma_start(out=x1chunk[:, lt4 * D:(lt4 + 1) * D],
                                in_=x1_hbm[lt * 128:(lt + 1) * 128, :]).then_inc(s["xld"].h, 16)
            P.on("sync", sy_x1)
            xld_tgt = s["xld"].inc(64)

            xst_prev = s["xst"].v
            def sc_sq(e, w=xld_tgt, wx=xst_prev, ch=ch, wb=inb_tgt):
                if ch == 0:
                    load_act(e, "natural_log_exp_and_others")
                    e.wait_ge(s["ld0"].h, wb)
                e.wait_ge(s["xld"].h, w)
                if wx > 0:
                    e.wait_ge(s["xst"].h, wx)
                for lt4 in range(4):
                    ins = e.activation(out=xsbuf[:], in_=x1chunk[:, lt4 * D:(lt4 + 1) * D],
                                       func=AF.Square, accum_out=g1(SQ, lt4))
                ins.then_inc(s["sq"].h, 1)
            P.on("scalar", sc_sq)
            sq_tgt = s["sq"].inc()

            def v_chain(e, w=xld_tgt, wsq=sq_tgt):
                e.wait_ge(s["xld"].h, w)
                for lt4 in range(4):
                    ins = e.reduce_sum(g1(SU, lt4), x1chunk[:, lt4 * D:(lt4 + 1) * D], axis=AX.X)
                VI(ins)
                e.wait_ge(s["sq"].h, wsq)
                VW(e)
                VI(e.tensor_scalar_mul(g4(M), g4(SU), 1.0 / D))
                VI(e.tensor_scalar_mul(g4(MSQ), g4(SQ), 1.0 / D))
                VW(e)
                VI(e.tensor_mul(g4(M2), g4(M), g4(M)))
                VW(e)
                VI(e.tensor_sub(g4(VAR), g4(MSQ), g4(M2)))
                VW(e)
                e.tensor_scalar_add(g4(VEPS), g4(VAR), EPS).then_inc(s["stats"].h, 1)
            P.on("vector", v_chain)
            st_tgt = s["stats"].inc()

            ln_tgt = s["sq"].inc()
            rsq_tgt = s["sq"].inc()
            def sc_rsq(e, w=st_tgt, wl=ln_tgt):
                e.wait_ge(s["stats"].h, w)
                e.activation(out=g4(LNV), in_=g4(VEPS), func=AF.Ln).then_inc(s["sq"].h, 1)
                e.wait_ge(s["sq"].h, wl)
                e.activation(out=g4(RR), in_=g4(LNV), func=AF.Exp,
                             scale=-0.5).then_inc(s["sq"].h, 1)
            P.on("scalar", sc_rsq)

            def v_chain2(e, w=rsq_tgt, ch=ch):
                e.wait_ge(s["sq"].h, w)
                VI(e.tensor_mul(statrow[:, 48:52], g4(M), g4(RR)))
                e.tensor_scalar_add(srow2[:, ch * 4:(ch + 1) * 4], g4(RR), 1.0)
                VW(e)
                e.tensor_scalar_mul(crow2[:, ch * 4:(ch + 1) * 4], statrow[:, 48:52],
                                    -1.0).then_inc(s["stats"].h, 1)
            P.on("vector", v_chain2)
            st_tgt = s["stats"].inc()

            for lt4 in range(4):
                lt = ch * 4 + lt4
                xst_prev2 = s["xst"].v

                def sc_t2(e, lt4=lt4, ch=ch, w=st_tgt, wx=xst_prev2):
                    e.wait_ge(s["stats"].h, w)
                    if wx > 0:
                        e.wait_ge(s["xst"].h, wx)
                    e.activation(out=xsbuf[:], in_=x1chunk[:, lt4 * D:(lt4 + 1) * D],
                                 func=AF.Identity,
                                 scale=g1(RR, lt4),
                                 bias=crow2[:, ch * 4 + lt4:ch * 4 + lt4 + 1]
                                 ).then_inc(s["xs"].h, 1)
                    e.activation(out=statrow[:, 78:79], in_=statrow[:, 2:3],
                                 func=AF.Identity).then_inc(s["xfree"].h, 1)
                P.on("scalar", sc_t2)
                xs_tgt = s["xs"].inc()
                s["xfree"].inc()

                def t_t2tr(e, w=xs_tgt, fr=s["xsfm"].v, first=(ch == 0 and lt4 == 0),
                           wi=inwt_done):
                    if first:
                        e.wait_ge(s["xsfm"].h, wi)
                    e.wait_ge(s["xs"].h, w)
                    if fr > 0:
                        e.wait_ge(s["xsfm"].h, fr)
                    for kt in range(8):
                        ins = e.transpose(pE[:, kt * 128:(kt + 1) * 128],
                                          xsbuf[:, kt * 128:(kt + 1) * 128], ident[:])
                    ins.then_inc(s["xst"].h, 1)
                P.on("tensor", t_t2tr)
                xst_tgt = s["xst"].inc()

                def v_t2ev(e, lt=lt, w=xst_tgt, wm=s["gmm"].v):
                    e.wait_ge(s["xst"].h, w)
                    if lt >= 2:
                        e.wait_ge(s["gmm"].h, wm - 1)
                    buf = t2fm[lt % 2]
                    for kt in range(8):
                        slot = pE[:, kt * 128:(kt + 1) * 128]
                        rev = bass.AP(slot.tensor, slot.offset + 127,
                                      [[slot.ap[0][0], 128], [-1, 128]])
                        ins = e.tensor_copy(buf[:, kt * 128:(kt + 1) * 128], rev)
                    ins.then_inc(s["xsfm"].h, 1)
                P.on("vector", v_t2ev)
                ev_tgt = s["xsfm"].inc()

                pcd = (pC, pD) if lt % 2 == 0 else (pA, pB)
                def t_inproj(e, lt=lt, w=ev_tgt, uv=s["uev"].v, pcd=pcd):
                    e.wait_ge(s["xsfm"].h, w)
                    if lt >= 2:
                        e.wait_ge(s["uev"].h, uv - 1)
                    buf = t2fm[lt % 2]
                    for kt in range(8):
                        e.matmul(pcd[0][:], buf[:, kt * 128:(kt + 1) * 128],
                                 in_WT[:, kt * 1024: kt * 1024 + 512],
                                 start=(kt == 0), stop=(kt == 7))
                    for kt in range(8):
                        ins = e.matmul(pcd[1][:], buf[:, kt * 128:(kt + 1) * 128],
                                       in_WT[:, kt * 1024 + 512: (kt + 1) * 1024],
                                       start=(kt == 0), stop=(kt == 7))
                    ins.then_inc(s["gmm"].h, 1)
                P.on("tensor", t_inproj)
                mm_tgt = s["gmm"].inc()

                def v_uev(e, lt=lt, w=mm_tgt, pcd=pcd):
                    e.wait_ge(s["gmm"].h, w)
                    c = lt
                    for half, bank in ((0, pcd[0]), (1, pcd[1])):
                        out_ap = bass.AP(u_T[:].tensor,
                                         u_T[:].offset + half * 4 * 4096 + c * 128,
                                         [[u_T[:].ap[0][0], 128], [4096, 4], [1, 128]])
                        in_ps = bass.AP(bank[:].tensor, bank[:].offset,
                                        [[bank[:].ap[0][0], 128], [128, 4], [1, 128]])
                        in_b = bass.AP(in_bB_sb.tensor, in_bB_sb.offset + half * 512,
                                       [[in_bB_sb.ap[0][0], 128], [128, 4], [1, 128]])
                        ins = e.tensor_add(out_ap, in_ps, in_b)
                    ins.then_inc(s["uev"].h, 1)
                P.on("vector", v_uev)
                s["uev"].inc()

        if DEV_STAGE == 2:
            def sy_udump(e, w=s["uev"].v):
                e.wait_ge(s["uev"].h, w)
                e.dma_start(out=uT_hbm, in_=u_T[:]).then_inc(s["x1st"].h, 16)
            P.on("sync", sy_udump)
            s["x1st"].inc(16)

    if DEV_STAGE >= 3:
        barrier(pre_waits=[("vector", "uev", s["uev"].v),
                           ("sync", "x1st", s["x1st"].v),
                           ("tensor", "xsfm", s["xsfm"].v)])

        uT_ap = w1g[:]
        qb_bf = qbuf[:].bitcast(BF16)
        xh_bf = xhalf[:].bitcast(BF16)
        scl_bf3 = sclbuf[:].bitcast(BF16)
        wgb = xs_fm[:]
        t0b = w2buf[:]
        ytst = h2_fm[:, 4096:8192]
        ypm = wtmp[:]
        smre = qb_bf
        smim_bufs = (xs_fm[:], scl_bf3[:, 0:4096])
        lqzr, lqzi = zfm[:, 0:256], zfm[:, 256:512]
        st_re = [sclbuf[:, 2048:2304], sclbuf[:, 2304:2560]]
        st_im = [sclbuf[:, 2560:2816], sclbuf[:, 2816:3072]]
        sctmp = sclbuf[:, 3328:3584]

        def ghre_sl(g, o, n):
            return h2f[:, g * 1024 + o: g * 1024 + o + n]

        def ghim_sl(g, o, n):
            buf = wtmp if g < 4 else xhalf
            return buf[:, (g % 4) * 1024 + o: (g % 4) * 1024 + o + n]

        def VI3(ins):
            s["vch"].inc()
            ins.then_inc(s["vch"].h, 1)

        def VW3(e):
            e.wait_ge(s["vch"].h, s["vch"].v)

        UST = uT_ap.ap[0][0]

        def v_lqz(e):
            VI3(e.tensor_copy(lqzr, sclbuf[:, LQTRE:LQTRE + 256]))
            e.tensor_copy(lqzi, sclbuf[:, LQTIM:LQTIM + 256]).then_inc(s["shf"].h, 1)
        P.on("vector", v_lqz)
        s["shf"].inc()

        prev_gmm = s["gmm"].v
        for g in range(NG):
            wg_tgts = []
            for hg in range(2):
                gate_mm = prev_gmm + hg
                def sy_wgl(e, g=g, hg=hg, wm=gate_mm):
                    if wm > 0:
                        e.wait_ge(s["gmm"].h, wm)
                    e.dma_start(out=wgb, in_=wg_hbm[g][:, hg * 4096:(hg + 1) * 4096]
                                ).then_inc(s["w2ld"].h, 16)
                P.on("sync", sy_wgl)
                wg_tgts.append(s["w2ld"].inc(16))

            round_tgts = []
            for tqB in range(2):
                ev_gate = s["uev"].v
                def t_g(e, g=g, tqB=tqB, wl=wg_tgts[tqB], we=ev_gate):
                    e.wait_ge(s["w2ld"].h, wl)
                    if we > 0:
                        e.wait_ge(s["uev"].h, we)
                    for tqL in range(16):
                        tq = tqB * 16 + tqL
                        for m in range(4):
                            hs = 4 * tq + m
                            hl = hs % 64
                            for comp, bank in ((0, pG), (1, pH)):
                                st = bass.AP(wgb.tensor, wgb.offset + comp * 2048 + hl,
                                             [[wgb.ap[0][0], 128], [64, 32]])
                                mv = bass.AP(uT_ap.tensor, uT_ap.offset + g * 4096 + hs,
                                             [[UST, 128], [128, 32]])
                                ins = e.matmul(bank[32 * m:32 * m + 32,
                                                    tqL * 32:(tqL + 1) * 32],
                                               st, mv, start=True, stop=True,
                                               tile_position=(0, 32 * m))
                    ins.then_inc(s["gmm"].h, 1)
                P.on("tensor", t_g)
                round_tgts.append(s["gmm"].inc())

                def v_gev(e, g=g, tqB=tqB, w=round_tgts[-1]):
                    e.wait_ge(s["gmm"].h, w)
                    e.tensor_copy(ghre_sl(g, tqB * 512, 512), pG[:])
                    e.tensor_copy(ghim_sl(g, tqB * 512, 512),
                                  pH[:]).then_inc(s["uev"].h, 1)
                P.on("vector", v_gev)
                s["uev"].inc()
            prev_gmm = round_tgts[-1]
        gev_done = s["uev"].v

        def v_scan(e, w=gev_done):
            e.wait_ge(s["uev"].h, w)
            e.memset(bass.AP(smre.tensor, smre.offset,
                             [[smre.ap[0][0], 128], [32, 256], [1, 1]]), 0.0)
            for gh in range(2):
                buf = smim_bufs[gh]
                e.memset(bass.AP(buf.tensor, buf.offset,
                                 [[buf.ap[0][0], 128], [32, 128], [1, 1]]), 0.0)
            gre = lambda c: bass.AP(h2f.tensor, h2f.offset + c,
                                    [[h2f.ap[0][0], 128], [1024, 8], [32, 32]])
            gimh = lambda b, c: bass.AP(b[:].tensor, b[:].offset + c,
                                        [[b[:].ap[0][0], 128], [1024, 4], [32, 32]])
            smre_st = lambda c: bass.AP(smre.tensor, smre.offset + c,
                                        [[smre.ap[0][0], 128], [1024, 8], [32, 32]])
            smim_st = lambda gh, c: bass.AP(smim_bufs[gh].tensor, smim_bufs[gh].offset + c,
                                            [[smim_bufs[gh].ap[0][0], 128], [1024, 4], [32, 32]])
            for c in range(31):
                new_r, old_r = st_re[(c + 1) % 2], st_re[c % 2]
                new_i, old_i = st_im[(c + 1) % 2], st_im[c % 2]
                if c == 0:
                    VI3(e.tensor_copy(new_r, gre(0)))
                    VI3(e.tensor_copy(new_i[:, 0:128], gimh(wtmp, 0)))
                    VI3(e.tensor_copy(new_i[:, 128:256], gimh(xhalf, 0)))
                    VW3(e)
                else:
                    VI3(e.tensor_mul(new_r, lqzr, old_r))
                    VI3(e.tensor_mul(sctmp, lqzi, old_i))
                    VW3(e)
                    VI3(e.tensor_sub(new_r, new_r, sctmp))
                    VI3(e.tensor_mul(new_i, lqzr, old_i))
                    VW3(e)
                    VI3(e.tensor_add(new_r, new_r, gre(c)))
                    VI3(e.tensor_mul(sctmp, lqzi, old_r))
                    VW3(e)
                    VI3(e.tensor_add(new_i, new_i, sctmp))
                    VW3(e)
                    VI3(e.tensor_add(new_i[:, 0:128], new_i[:, 0:128], gimh(wtmp, c)))
                    VI3(e.tensor_add(new_i[:, 128:256], new_i[:, 128:256], gimh(xhalf, c)))
                    VW3(e)
                VI3(e.tensor_copy(smre_st(c + 1), new_r))
                VI3(e.tensor_copy(smim_st(0, c + 1), new_i[:, 0:128]))
                ins = e.tensor_copy(smim_st(1, c + 1), new_i[:, 128:256])
                if c == 30:
                    ins.then_inc(s["shf"].h, 1)
                else:
                    VI3(ins)
                    VW3(e)
        P.on("vector", v_scan)
        scan_tgt = s["shf"].inc()

        prev_cnv_all = s["cnv"].v
        prev_yT_last = s["zt"].v
        prev_yst = 0
        t0_load_tgt = {}
        t0_slot_user = {}
        for g in range(NG):
            def sy_wcl(e, g=g, wc=prev_cnv_all, ws=scan_tgt):
                e.wait_ge(s["shf"].h, ws)
                if wc > 0:
                    e.wait_ge(s["cnv"].h, wc)
                e.dma_start(out=xh_bf[:, 0:4096],
                            in_=wcre_hbm[:, g * 4096:(g + 1) * 4096]).then_inc(s["w2ld"].h, 16)
                e.dma_start(out=xh_bf[:, 4096:8192],
                            in_=wcim_hbm[:, g * 4096:(g + 1) * 4096]).then_inc(s["w2ld"].h, 16)
            def emit_t0(qq, gate):
                slot = qq % 2
                def sy_t0(e, g=g, qq=qq, slot=slot, wc=gate):
                    if wc:
                        e.wait_ge(s["cnv"].h, wc)
                    src_ap = bass.AP(kT_hbm.tensor, (4 * g + qq) * 8192 + 32,
                                     [[32, 128], [1, 4096]])
                    e.dma_start(out=t0b[:, slot * 4096:(slot + 1) * 4096],
                                in_=src_ap).then_inc(s["w2ld"].h, 16)
                P.on("sync", sy_t0)
                t0_load_tgt[(g, qq)] = s["w2ld"].inc(16)

            emit_t0(0, t0_slot_user.get(0))
            emit_t0(1, t0_slot_user.get(1))
            P.on("sync", sy_wcl)
            wc_tgt = s["w2ld"].inc(32)

            bank_ev = {0: 0, 1: 0}
            yev_tgts = []
            for hB in range(8):
                bank = pC if hB % 2 == 0 else pD
                qq = hB // 2
                slot = qq % 2
                smim_b = smim_bufs[0] if g < 4 else smim_bufs[1]
                smim_base = (g % 4) * 1024
                def t_y(e, g=g, hB=hB, bank=bank, slot=slot, ws=scan_tgt,
                        wt=t0_load_tgt[(g, qq)], wwc=wc_tgt,
                        wb=bank_ev[hB % 2], smim_b=smim_b, smim_base=smim_base):
                    if hB == 0:
                        e.wait_ge(s["shf"].h, ws)
                        e.wait_ge(s["w2ld"].h, wwc)
                    e.wait_ge(s["w2ld"].h, wt)
                    if wb > 0:
                        e.wait_ge(s["yev"].h, wb)
                    for hL in range(16):
                        hs = hB * 16 + hL
                        hq = hs % 32
                        m = hs % 4
                        tq = hs // 4
                        out = bank[:, hL * 32:(hL + 1) * 32]
                        st0 = bass.AP(t0b.tensor, t0b.offset + slot * 4096 + hq,
                                      [[t0b.ap[0][0], 128], [32, 128]])
                        mvu = bass.AP(uT_ap.tensor, uT_ap.offset + g * 4096 + hs,
                                      [[UST, 128], [128, 32]])
                        e.matmul(out, st0, mvu, start=True, stop=False)
                        stre = xh_bf[32 * m:32 * m + 32, tq * 128:(tq + 1) * 128]
                        stim = xh_bf[32 * m:32 * m + 32,
                                     4096 + tq * 128: 4096 + (tq + 1) * 128]
                        mre = smre[32 * m:32 * m + 32,
                                   g * 1024 + tq * 32: g * 1024 + (tq + 1) * 32]
                        mim = smim_b[32 * m:32 * m + 32,
                                     smim_base + tq * 32: smim_base + (tq + 1) * 32]
                        e.matmul(out, stre, mre, start=False, stop=False,
                                 tile_position=(32 * m, 0))
                        ins = e.matmul(out, stim, mim, start=False, stop=True,
                                       tile_position=(32 * m, 0))
                    ins.then_inc(s["cnv"].h, 1)
                P.on("tensor", t_y)
                cnv_tgt = s["cnv"].inc()

                def v_yev(e, g=g, hB=hB, bank=bank, w=cnv_tgt,
                          wz=prev_yT_last if hB == 0 else 0):
                    e.wait_ge(s["cnv"].h, w)
                    if wz > 0:
                        e.wait_ge(s["zt"].h, wz)
                    e.tensor_copy(ypm[:, hB * 512:(hB + 1) * 512],
                                  bank[:]).then_inc(s["yev"].h, 1)
                P.on("vector", v_yev)
                bank_ev[hB % 2] = s["yev"].inc()
                yev_tgts.append(bank_ev[hB % 2])
                if hB % 2 == 1:
                    t0_slot_user[qq % 2] = cnv_tgt
                    if qq < 2:
                        emit_t0(qq + 2, cnv_tgt)
            prev_cnv_all = s["cnv"].v
            ypm_done = yev_tgts[-1]

            tr_bank_gate = {0: 0, 1: 0}
            for r in range(8):
                bank = pF if r % 2 == 0 else pA
                def t_ytr(e, g=g, r=r, bank=bank, w=ypm_done,
                          wg2=tr_bank_gate[r % 2]):
                    e.wait_ge(s["yev"].h, w)
                    if wg2 > 0:
                        e.wait_ge(s["h2"].h, wg2)
                    for j in range(4):
                        c = r * 4 + j
                        inap = bass.AP(ypm.tensor, ypm.offset + c,
                                       [[ypm.ap[0][0], 128], [32, 128]])
                        ins = e.transpose(bank[:, j * 128:(j + 1) * 128],
                                          inap, ident32[:])
                    ins.then_inc(s["zt"].h, 1)
                P.on("tensor", t_ytr)
                ytr_tgt = s["zt"].inc()

                def sc_gelu(e, g=g, r=r, bank=bank, w=ytr_tgt, wst=prev_yst):
                    if g == 0 and r == 0:
                        load_act(e, "gelu_and_others")
                    if r == 0 and wst > 0:
                        e.wait_ge(s["ld0"].h, wst)
                    e.wait_ge(s["zt"].h, w)
                    for j in range(4):
                        c = r * 4 + j
                        ins = e.activation(out=ytst[:, c * 128:(c + 1) * 128],
                                           in_=bank[:, j * 128:(j + 1) * 128],
                                           func=AF.Gelu)
                    ins.then_inc(s["h2"].h, 1)
                P.on("scalar", sc_gelu)
                tr_bank_gate[r % 2] = s["h2"].inc()
            prev_yT_last = s["zt"].v
            gelu_done = s["h2"].v

            def sy_yst(e, g=g, w=gelu_done):
                e.wait_ge(s["h2"].h, w)
                e.dma_start(out=yhm_hbm[g * 128:(g + 1) * 128, :],
                            in_=ytst).then_inc(s["ld0"].h, 16)
            P.on("sync", sy_yst)
            prev_yst = s["ld0"].inc(16)
        conv_store_done = prev_yst

    if DEV_STAGE >= 4 and DEV_STAGE != 25:
        barrier(pre_waits=[("sync", "ld0", s["ld0"].v),
                           ("vector", "shf", s["shf"].v),
                           ("scalar", "h2", s["h2"].v),
                           ("tensor", "h2", s["h2"].v)])
        qb_bf3 = qbuf[:].bitcast(BF16)
        gluw_sb = h2_fm[:]
        outw_sb = w2buf[:]
        ybuf = xs_fm[:]
        y2buf = qb_bf3[:, 0:4096]
        x1ch3 = sclbuf[:, 0:4096]
        atmp = wtmp[:, 0:512]
        sgtmp = zfm[:, 0:512]
        xhst = wtmp[:, 1024:2048]
        outb_sb = wtmp[:, 2048:3072]
        x2st = [xhalf[:, 0:1024], xhalf[:, 1024:2048]]
        x2_store_tgt = {0: 0, 1: 0}

        def sy_w1rl(e):
            for gi in range(NG):
                e.dma_start(out=w1g[:, gi * FF:(gi + 1) * FF],
                            in_=w1g_hbm[:, gi * FF:(gi + 1) * FF]).then_inc(s["ld0"].h, 16)
        P.on("sync", sy_w1rl)
        w1rl_tgt = s["ld0"].inc(16 * 8)

        def sy_p3w(e):
            e.dma_start(out=gluw_sb, in_=gluWT_hbm).then_inc(s["ld0"].h, 16)
            e.dma_start(out=outw_sb, in_=outWT_hbm).then_inc(s["ld0"].h, 16)
            e.dma_start(out=outb_sb, in_=out_bB).then_inc(s["ld0"].h, 16)
        P.on("sync", sy_p3w)
        p3w_tgt = s["ld0"].inc(48)

        for lch in range(8):
            zdone = s["zev"].v
            def sy_yld(e, lch=lch, wz=zdone):
                if wz > 0:
                    e.wait_ge(s["zev"].h, wz)
                for gi in range(NG):
                    e.dma_start(out=ybuf[:, gi * 512:(gi + 1) * 512],
                                in_=yhm_hbm[gi * 128:(gi + 1) * 128,
                                            lch * 512:(lch + 1) * 512]
                                ).then_inc(s["xld"].h, 16)
            P.on("sync", sy_yld)
            yld_tgt = s["xld"].inc(16 * 8)

            xapp_done = s["xs"].v
            def sy_x1ld(e, lch=lch, wx=xapp_done):
                if wx > 0:
                    e.wait_ge(s["xs"].h, wx)
                for lt4 in range(4):
                    lt = lch * 4 + lt4
                    e.dma_start(out=x1ch3[:, lt4 * 1024:(lt4 + 1) * 1024],
                                in_=x1_hbm[lt * 128:(lt + 1) * 128, :]
                                ).then_inc(s["xld"].h, 16)
            P.on("sync", sy_x1ld)
            x1ld_tgt = s["xld"].inc(64)

            for ot in range(8):
                y2_done = s["zev"].v
                def t_z(e, lch=lch, ot=ot, w=yld_tgt, wp=p3w_tgt, wy=y2_done):
                    if ot == 0:
                        e.wait_ge(s["ld0"].h, wp)
                        e.wait_ge(s["xld"].h, w)
                    if wy > 0:
                        e.wait_ge(s["zev"].h, wy)
                    for half, bank in ((0, pA), (1, pB)):
                        oti = ot + 8 * half
                        for gi in range(NG):
                            ins = e.matmul(bank[:],
                                           gluw_sb[:, oti * 1024 + gi * 128:
                                                   oti * 1024 + (gi + 1) * 128],
                                           ybuf[:, gi * 512:(gi + 1) * 512],
                                           start=(gi == 0), stop=(gi == 7))
                    ins.then_inc(s["fc1"].h, 1)
                P.on("tensor", t_z)
                z_tgt = s["fc1"].inc()

                def sc_sig(e, lch=lch, ot=ot, w=z_tgt):
                    if lch == 0 and ot == 0:
                        load_act(e, "sigmoid_and_others")
                    e.wait_ge(s["fc1"].h, w)
                    e.activation(out=sgtmp, in_=pB[:], func=AF.Sigmoid,
                                 bias=glub_sb[:, ot + 8:ot + 9])
                    e.activation(out=atmp, in_=pA[:], func=AF.Identity,
                                 bias=glub_sb[:, ot:ot + 1]).then_inc(s["sq"].h, 1)
                P.on("scalar", sc_sig)
                sig_tgt = s["sq"].inc()

                def v_y2(e, ot=ot, w=sig_tgt):
                    e.wait_ge(s["sq"].h, w)
                    e.tensor_mul(y2buf[:, ot * 512:(ot + 1) * 512], atmp,
                                 sgtmp).then_inc(s["zev"].h, 1)
                P.on("vector", v_y2)
                s["zev"].inc()
            y2_all = s["zev"].v

            for lt4 in range(4):
                lt = lch * 4 + lt4
                def sc_xapp(e, lch=lch, lt4=lt4, lt=lt, w=x1ld_tgt, wy=s["x1rdy"].v):
                    e.wait_ge(s["xld"].h, w)
                    if wy > 0:
                        e.wait_ge(s["x1rdy"].h, wy)
                    e.activation(out=xhst, in_=x1ch3[:, lt4 * 1024:(lt4 + 1) * 1024],
                                 func=AF.Identity, scale=srow2[:, lt:lt + 1],
                                 bias=crow2[:, lt:lt + 1]).then_inc(s["xs"].h, 1)
                P.on("scalar", sc_xapp)
                xapp_tgt = s["xs"].inc()

                op_tgts = []
                for dh in range(2):
                    bank = pC if dh == 0 else pD
                    x2done = s["x1rdy"].v
                    def t_op(e, lt4=lt4, dh=dh, bank=bank, wy=y2_all, wx=x2done):
                        if lt4 == 0 and dh == 0:
                            e.wait_ge(s["zev"].h, wy)
                        if wx > 0:
                            e.wait_ge(s["x1rdy"].h, wx)
                        for ob in range(NG):
                            mvw = bass.AP(outw_sb.tensor,
                                          outw_sb.offset + 4 * dh * 1024 + ob * 128,
                                          [[outw_sb.ap[0][0], 128], [1024, 4], [1, 128]])
                            ins = e.matmul(bank[:],
                                           y2buf[:, ob * 512 + lt4 * 128:
                                                 ob * 512 + (lt4 + 1) * 128],
                                           mvw, start=(ob == 0), stop=(ob == 7))
                        ins.then_inc(s["fc2"].h, 1)
                    P.on("tensor", t_op)
                    op_tgts.append(s["fc2"].inc())

                def v_x2(e, lt4=lt4, w=op_tgts, wx=xapp_tgt,
                         wst=x2_store_tgt[lt4 % 2]):
                    e.wait_ge(s["xs"].h, wx)
                    if wst > 0:
                        e.wait_ge(s["x1st"].h, wst)
                    x2t = x2st[lt4 % 2]
                    e.wait_ge(s["fc2"].h, w[0])
                    VI3(e.tensor_add(x2t[:, 0:512], pC[:], outb_sb[:, 0:512]))
                    e.wait_ge(s["fc2"].h, w[1])
                    VI3(e.tensor_add(x2t[:, 512:1024], pD[:], outb_sb[:, 512:1024]))
                    VW3(e)
                    VI3(e.tensor_add(x2t[:, 0:512], x2t[:, 0:512], xhst[:, 0:512]))
                    e.tensor_add(x2t[:, 512:1024], x2t[:, 512:1024],
                                 xhst[:, 512:1024]).then_inc(s["x1rdy"].h, 1)
                P.on("vector", v_x2)
                x2r_tgt = s["x1rdy"].inc()

                def sy_x2(e, lt=lt, lt4=lt4, w=x2r_tgt):
                    e.wait_ge(s["x1rdy"].h, w)
                    e.dma_start(out=x2_hbm[lt * 128:(lt + 1) * 128, :],
                                in_=x2st[lt4 % 2]).then_inc(s["x1st"].h, 16)
                P.on("sync", sy_x2)
                x2_store_tgt[lt4 % 2] = s["x1st"].inc(16)

        if DEV_STAGE == 4:
            pass

    if DEV_STAGE >= 5 and DEV_STAGE != 25:
        barrier(pre_waits=[("sync", "x1st", s["x1st"].v),
                           ("vector", "x1rdy", s["x1rdy"].v),
                           ("tensor", "x1rdy", s["x1rdy"].v)])

        emit_ffn(x2_hbm, out_ext, ("ld0", w1rl_tgt))

    def sy_fin(e, w=s["x1st"].v, wl=s["ld0"].v):
        e.wait_ge(s["x1st"].h, w)
        e.wait_ge(s["ld0"].h, wl)
    P.on("sync", sy_fin)

    with nc.Block() as blk:
        blk.sync(lambda e: [f(e) for f in P.ops["sync"]])
        blk.gpsimd(lambda e: [f(e) for f in P.ops["gpsimd"]])
        blk.vector(lambda e: [f(e) for f in P.ops["vector"]])
        blk.scalar(lambda e: [f(e) for f in P.ops["scalar"]])
        blk.tensor(lambda e: [f(e) for f in P.ops["tensor"]])
    es_p01.close()
    es.close()
    return nc


_cached = {}


def kernel(**inputs):
    nc = _cached.get("nc")
    if nc is None:
        nc = build_program()
        _cached["nc"] = nc

    f32 = lambda a: np.ascontiguousarray(np.asarray(a, dtype=np.float32))
    i32 = lambda a: np.ascontiguousarray(np.asarray(a, dtype=np.int32))
    x = f32(inputs["L0_attn_x"])

    fc1_scales = f32(inputs["L0_ffn_fc1_scales"])
    fc2_scales = f32(inputs["L0_ffn_fc2_scales"])
    fc2_q = i32(inputs["L0_ffn_fc2_q"])
    fc2_qc = np.ascontiguousarray(
        fc2_q.reshape(32, 128, 8, 128).transpose(2, 1, 0, 3).reshape(8, 128, 32 * 128))
    fc2_scales_c = np.ascontiguousarray(np.broadcast_to(
        fc2_scales.reshape(32, 8, 128).transpose(1, 0, 2)[:, None, :, :],
        (8, 128, 32, 128)).reshape(8, 128, FF))
    fc1_scales_b = np.ascontiguousarray(
        np.broadcast_to(fc1_scales[:, None, :], (8, 128, FF)))

    col = lambda a, n: np.ascontiguousarray(f32(a).reshape(n, 128).T)
    bcast = lambda a: np.ascontiguousarray(
        np.broadcast_to(f32(a)[None, :], (128, f32(a).shape[0])))

    grid = lambda a: np.ascontiguousarray(
        f32(a).reshape(8, 128, 32).transpose(1, 0, 2).reshape(128, 256))
    C = f32(inputs["L0_attn_C"])
    common = dict(
        in_W=f32(inputs["L0_attn_in_W"]),
        in_bB=bcast(inputs["L0_attn_in_b"]),
        dt_col=col(inputs["L0_attn_log_dt"], 8),
        dsk_col=col(inputs["L0_attn_D_skip"], 8),
        lar_g=grid(inputs["L0_attn_log_A_real"]),
        aim_g=grid(inputs["L0_attn_A_imag"]),
        c_re_g=grid(C[..., 0]),
        c_im_g=grid(C[..., 1]),
        glu_W=f32(inputs["L0_attn_glu_W"]),
        glu_b_col=col(inputs["L0_attn_glu_b"], 16),
        out_W=f32(inputs["L0_attn_out_W"]),
        out_bB=bcast(inputs["L0_attn_out_b"]),
        gamma_col=col(inputs["L0_ffn_ln_gamma"], 8),
        beta_col=col(inputs["L0_ffn_ln_beta"], 8),
        fc1_q=i32(inputs["L0_ffn_fc1_q"]),
        fc1_scales_b=fc1_scales_b,
        fc1_bias_col=col(inputs["L0_ffn_fc1_bias"], 32),
        fc2_qc=fc2_qc,
        fc2_scales_c=fc2_scales_c,
        fc2_bias_col=col(inputs["L0_ffn_fc2_bias"], 8),
    )

    in_maps = []
    for b in range(NCORES):
        m = dict(common)
        m["x"] = np.ascontiguousarray(x[b])
        in_maps.append(m)

    res = run_bass_kernel_spmd(nc, in_maps, core_ids=list(range(NCORES)))
    if DEV_STAGE == 25:
        return {k: np.asarray(res.results[0][k]) for k in
                ("kT_hbm", "wg_hbm", "wcre_hbm", "wcim_hbm", "lamq_hbm",
                 "gluWT_hbm", "outWT_hbm")}
    key = "out" if DEV_STAGE == 5 else {1: "dbg_x1", 2: "dbg_u", 3: "dbg_y", 4: "dbg_x2"}[DEV_STAGE]
    outs = [np.asarray(res.results[b][key]) for b in range(NCORES)]
    return np.stack(outs, 0)


# version 07
# speedup vs baseline: 1.0527x; 1.0059x over previous
import os
import sys
import numpy as np
from contextlib import ExitStack

sys.path.insert(0, "/opt/trn_rl_repo")
import concourse.bass as bass
import concourse.mybir as mybir
from concourse.bass_utils import run_bass_kernel_spmd

F32 = mybir.dt.float32
BF16 = mybir.dt.bfloat16
I32 = mybir.dt.int32
AF = mybir.ActivationFunctionType
ALU = mybir.AluOpType
AX = mybir.AxisListType

NCORES = 8
L = 4096
D = 1024
FF = 4096
NM = 32
Q = 128
NCH = L // Q
NT = L // 128
NG = D // 128
EPS = 1e-5
PI = float(np.pi)

DEV_STAGE = int(os.environ.get("MK_STAGE", "5"))


class Sem:
    def __init__(self, h):
        self.h = h
        self.v = 0

    def inc(self, n=1):
        self.v += n
        return self.v


class Prog:
    def __init__(self):
        self.ops = {k: [] for k in ("sync", "gpsimd", "vector", "scalar", "tensor")}

    def on(self, eng, fn):
        self.ops[eng].append(fn)


def rsqrt_ops(e, out, v, scratch):
    y0, t = scratch[:, 0:1], scratch[:, 1:2]
    vi = v.bitcast(I32)
    yi = y0.bitcast(I32)
    e.tensor_scalar(out=yi, in0=vi, scalar1=1, scalar2=None,
                    op0=ALU.logical_shift_right)
    e.tensor_scalar(out=yi, in0=yi, scalar1=-1, scalar2=0x5F3759DF,
                    op0=ALU.mult, op1=ALU.add)
    for _ in range(3):
        e.tensor_mul(t, y0, y0)
        e.tensor_mul(t, t, v)
        e.tensor_scalar(out=t, in0=t, scalar1=-0.5, scalar2=1.5,
                        op0=ALU.mult, op1=ALU.add)
        e.tensor_mul(y0, y0, t)
    e.tensor_copy(out, y0)


def build_program():
    nc = bass.Bass("TRN2", target_bir_lowering=False,
                   detect_race_conditions=bool(int(os.environ.get("MK_RACE", "0"))))
    P = Prog()

    from concourse.hw_specs import get_activation_tables
    act_sets = {name: i for i, name in enumerate(get_activation_tables(nc.m.arch))}

    def load_act(e, set_name):
        e.add_instruction(mybir.InstLoadActFuncSet(
            name=nc.get_next_instruction_name(),
            act_func_set_id=act_sets[set_name], ins=[], outs=[]))

    dI = lambda n, sh, dt=F32: nc.dram_tensor(n, sh, dt, kind="ExternalInput").ap()
    x_in = dI("x", [L, D])
    in_W = dI("in_W", [D, D])
    in_bB = dI("in_bB", [128, D])
    dt_col = dI("dt_col", [128, 8])
    dsk_col = dI("dsk_col", [128, 8])
    lar_g = dI("lar_g", [128, 256])
    aim_g = dI("aim_g", [128, 256])
    c_re_g = dI("c_re_g", [128, 256])
    c_im_g = dI("c_im_g", [128, 256])
    glu_W = dI("glu_W", [2 * D, D])
    glu_b_col = dI("glu_b_col", [128, 16])
    out_W = dI("out_W", [D, D])
    out_bB = dI("out_bB", [128, D])
    gamma_col = dI("gamma_col", [128, 8])
    beta_col = dI("beta_col", [128, 8])
    fc1_q = dI("fc1_q", [D, FF], I32)
    fc1_scales_b = dI("fc1_scales_b", [8, 128, FF])
    fc1_bias_col = dI("fc1_bias_col", [128, 32])
    fc2_qc = dI("fc2_qc", [8, 128, 32 * 128], I32)
    fc2_scales_c = dI("fc2_scales_c", [8, 128, FF])
    fc2_bias_col = dI("fc2_bias_col", [128, 8])

    out_ext = nc.dram_tensor("out", [L, D], F32, kind="ExternalOutput").ap()

    def dS(name, sh, dt=F32, dbg=False):
        kind = "ExternalOutput" if dbg else "Internal"
        return nc.dram_tensor(name, sh, dt, kind=kind).ap()

    x1_hbm = dS("dbg_x1" if DEV_STAGE == 1 else "x1_hbm", [L, D], F32, DEV_STAGE == 1)
    x2_hbm = dS("dbg_x2" if DEV_STAGE == 4 else "x2_hbm", [L, D], F32, DEV_STAGE == 4)
    uT_hbm = dS("dbg_u" if DEV_STAGE == 2 else "uT_hbm", [128, NCH * D], BF16, DEV_STAGE == 2)

    w1g_hbm = dS("w1g_hbm", [128, NG * FF], BF16)
    w2q_hbm = dS("w2q_hbm", [8, 128, 32 * 128], BF16)
    brow_hbm = dS("brow_hbm", [1, FF])
    TDBG = DEV_STAGE == 25
    kT_hbm = dS("kT_hbm", [32, 256, 32], BF16, TDBG)
    wg_hbm = dS("wg_hbm", [NG, 128, 8192], BF16, TDBG)
    wcre_hbm = dS("wcre_hbm", [128, 32768], BF16, TDBG)
    wcim_hbm = dS("wcim_hbm", [128, 32768], BF16, TDBG)
    lamq_hbm = dS("lamq_hbm", [128, 512], F32, TDBG)
    gluWT_hbm = dS("gluWT_hbm", [128, 16384], BF16, TDBG)
    outWT_hbm = dS("outWT_hbm", [128, 8192], BF16, TDBG)
    yhm_hbm = dS("dbg_y" if DEV_STAGE == 3 else "yhm_hbm", [D, L], BF16, DEV_STAGE == 3)

    es = ExitStack()
    es_p01 = ExitStack()
    SB = lambda n, sh, dt=F32: es.enter_context(nc.sbuf_tensor(n, sh, dt))
    SB1 = lambda n, sh, dt=F32: es_p01.enter_context(nc.sbuf_tensor(n, sh, dt, side="right"))
    PS = lambda n, sh, dt=F32: es.enter_context(nc.psum_tensor(n, sh, dt))

    pA = PS("pA", [128, 512])
    pB = PS("pB", [128, 512])
    pC = PS("pC", [128, 512])
    pD = PS("pD", [128, 512])
    pE = PS("pE", [128, 1024], BF16)
    pF = PS("pF", [128, 512])
    pG = PS("pG", [128, 512])
    pH = PS("pH", [128, 512])

    ident = SB("ident", [128, 128], BF16)
    ident32 = SB("ident32", [128, 128])
    bias_sb = SB("bias_sb", [128, 64])
    b1p_col = SB("b1p_col", [128, 32])
    b2q_col = SB("b2q_col", [128, 8])
    statrow = SB("statrow", [128, 80])
    srow2 = SB("srow2", [128, NT])
    crow2 = SB("crow2", [128, NT])
    glub_sb = SB("glub_sb", [128, 16])

    w1g = SB1("w1g", [128, NG * FF], BF16)
    qbuf = SB1("qbuf", [128, FF])
    sclbuf = SB1("sclbuf", [128, FF])
    wtmp = SB1("wtmp", [128, FF])
    sqscrap = wtmp[:, 0:D]
    xhalf = SB1("xhalf", [128, 4 * D])
    xsbuf = SB1("xsbuf", [128, D], BF16)
    xs_fm = SB1("xs_fm", [128, NG * 512], BF16)
    h2_fm = SB1("h2_fm", [128, 32 * 512], BF16)
    w2buf = SB1("w2buf", [128, 2 * 4096], BF16)
    zfm = SB1("zfm", [128, 512])
    x1part = [qbuf[:, i * D:(i + 1) * D] for i in range(2)] + \
             [sclbuf[:, i * D:(i + 1) * D] for i in range(2)]
    wbf = h2_fm[:, 0:FF]
    brow = xhalf[0:1, 0:FF]

    names = ("ld0", "deq", "w1", "bw", "bwc", "xld", "xfree", "sq", "stats", "xs",
             "xst", "xsfm", "fc1", "h2", "w2ld", "w2free", "fc2", "zev", "zt",
             "ztfree", "x1rdy", "x1st", "ph", "gld", "gdma", "uev", "gmm", "scn",
             "shf", "cnv", "yev", "glu", "op", "vch")
    s = {n: Sem(es.enter_context(nc.semaphore(n))) for n in names}

    ENGS = ("sync", "gpsimd", "vector", "scalar", "tensor")

    def barrier(pre_waits=()):
        for eng, sem_name, val in pre_waits:
            P.on(eng, lambda e, sn=sem_name, v=val: e.wait_ge(s[sn].h, v))
        P.on("vector", lambda e: e.tensor_copy(statrow[:, 76:77],
                                               statrow[:, 2:3]).then_inc(s["ph"].h, 1))
        P.on("scalar", lambda e: e.activation(out=statrow[:, 77:78], in_=statrow[:, 2:3],
                                              func=AF.Identity).then_inc(s["ph"].h, 1))
        P.on("tensor", lambda e: e.matmul(pF[0:1, 0:1], ident32[:, 0:1], ident32[:, 0:1],
                                          start=True, stop=True).then_inc(s["ph"].h, 1))
        P.on("gpsimd", lambda e: e.nop().then_inc(s["ph"].h, 1))
        P.on("sync", lambda e: e.nop().then_inc(s["ph"].h, 1))
        s["ph"].inc(5)
        tgt = s["ph"].v
        for eng in ENGS:
            P.on(eng, lambda e, t=tgt: e.wait_ge(s["ph"].h, t))

    def g_init(e):
        e.memset(ident[:], 0.0).then_inc(s["gld"].h, 1)
        e.memset(ident32[:], 0.0).then_inc(s["gld"].h, 1)
        e.wait_ge(s["gld"].h, 2)
        e.affine_select(out=ident[:], in_=ident[:], compare_op=ALU.not_equal,
                        fill=1.0, base=0, pattern=[[-1, 128]], channel_multiplier=1)
        e.affine_select(out=ident32[:], in_=ident32[:], compare_op=ALU.not_equal,
                        fill=1.0, base=0, pattern=[[-1, 128]], channel_multiplier=1)
        e.nop().then_inc(s["gld"].h, 2)
    P.on("gpsimd", g_init)
    s["gld"].inc(4)
    ident_ready = s["gld"].v

    def sy_small(e):
        e.dma_start(out=bias_sb[:, 0:8], in_=gamma_col).then_inc(s["ld0"].h, 16)
        e.dma_start(out=bias_sb[:, 8:16], in_=beta_col).then_inc(s["ld0"].h, 16)
        e.dma_start(out=bias_sb[:, 16:48], in_=fc1_bias_col).then_inc(s["ld0"].h, 16)
        e.dma_start(out=bias_sb[:, 48:56], in_=fc2_bias_col).then_inc(s["ld0"].h, 16)
        e.dma_start(out=glub_sb[:], in_=glu_b_col).then_inc(s["ld0"].h, 16)
    P.on("sync", sy_small)
    s["ld0"].inc(16 * 5)

    for g in range(NG):
        qsl = qbuf[:]
        ssl = sclbuf[:]
        if g >= 1:
            w_prev = s["w1"].v
            P.on("gpsimd", lambda e, w=w_prev: e.wait_ge(s["w1"].h, w))
            P.on("sync", lambda e, w=w_prev: e.wait_ge(s["w1"].h, w))
        def g_ld(e, qsl=qsl, g=g, wp=s["gdma"].v):
            if wp:
                e.wait_ge(s["gdma"].h, wp)
            e.dma_start(out=qsl, in_=fc1_q[g * 128:(g + 1) * 128, :]).then_inc(s["gdma"].h, 16)
        P.on("gpsimd", g_ld)
        def sy_ld(e, ssl=ssl, g=g):
            e.dma_start(out=ssl, in_=fc1_scales_b[g]).then_inc(s["ld0"].h, 16)
        P.on("sync", sy_ld)
        gq_tgt = s["gdma"].inc(16)
        ld_tgt = s["ld0"].inc(16)

        def v_deq(e, qsl=qsl, ssl=ssl, w=ld_tgt, wg=gq_tgt):
            e.wait_ge(s["ld0"].h, w)
            e.wait_ge(s["gdma"].h, wg)
            e.scalar_tensor_tensor(out=wtmp[:], in0=qsl, scalar=-128.0,
                                   in1=ssl, op0=ALU.add, op1=ALU.mult).then_inc(s["w1"].h, 1)
        w1_tgt = s["w1"].inc()
        P.on("vector", v_deq)

        for fch in range(8):
            bwc_need = s["bwc"].v
            def t_bw(e, g=g, fch=fch, w=w1_tgt, bn=bwc_need):
                if fch == 0:
                    e.wait_ge(s["w1"].h, w)
                if bn > 0:
                    e.wait_ge(s["bwc"].h, bn)
                e.matmul(pF[0:1, :], bias_sb[:, 8 + g:9 + g],
                         wtmp[:, fch * 512:(fch + 1) * 512],
                         start=True, stop=True).then_inc(s["bw"].h, 1)
            P.on("tensor", t_bw)
            bw_tgt = s["bw"].inc()

            def v_bw(e, g=g, fch=fch, w=bw_tgt):
                e.wait_ge(s["bw"].h, w)
                if g == 0:
                    e.tensor_copy(brow[0:1, fch * 512:(fch + 1) * 512],
                                  pF[0:1, :]).then_inc(s["bwc"].h, 1)
                else:
                    e.tensor_add(brow[0:1, fch * 512:(fch + 1) * 512],
                                 brow[0:1, fch * 512:(fch + 1) * 512],
                                 pF[0:1, :]).then_inc(s["bwc"].h, 1)
            P.on("vector", v_bw)
            s["bwc"].inc()

        def sc_gam(e, g=g, w=w1_tgt):
            if g == 0:
                load_act(e, "natural_log_exp_and_others")
            e.wait_ge(s["w1"].h, w)
            e.activation(out=w1g[:, g * FF:(g + 1) * FF], in_=wtmp[:],
                         func=AF.Identity,
                         scale=bias_sb[:, g:g + 1]).then_inc(s["deq"].h, 1)
        P.on("scalar", sc_gam)
        deq_tgt = s["deq"].inc()

        P.on("vector", lambda e, w=deq_tgt: e.wait_ge(s["deq"].h, w))

        def sy_w1out(e, g=g, w=deq_tgt):
            e.wait_ge(s["deq"].h, w)
            e.dma_start(out=w1g_hbm[:, g * FF:(g + 1) * FF],
                        in_=w1g[:, g * FF:(g + 1) * FF]).then_inc(s["ld0"].h, 16)
        P.on("sync", sy_w1out)
        s["ld0"].inc(16)

    def g_rowcol(e, w=s["bwc"].v, wp=s["gdma"].v):
        e.wait_ge(s["bwc"].h, w)
        if wp:
            e.wait_ge(s["gdma"].h, wp)
        e.dma_start(out=brow_hbm, in_=brow).then_inc(s["gdma"].h, 16)
    P.on("gpsimd", g_rowcol)
    st1 = s["gdma"].inc(16)

    brrl = h2_fm[:].bitcast(F32)[:, 4096:4224]
    def g_rowcol2(e, w=st1):
        e.wait_ge(s["gdma"].h, w)
        src = bass.AP(brow_hbm.tensor, brow_hbm.offset, [[128, 32], [1, 128]])
        e.dma_start(out=brrl[0:32, :], in_=src).then_inc(s["gdma"].h, 16)
    P.on("gpsimd", g_rowcol2)
    rc_tgt = s["gdma"].inc(16)

    def t_b1p(e, w=rc_tgt):
        e.wait_ge(s["gdma"].h, w)
        e.transpose(pG[:, 0:128], brrl, ident32[:]).then_inc(s["zt"].h, 1)
    P.on("tensor", t_b1p)
    b1pt_tgt = s["zt"].inc()

    def v_b1p(e, w=b1pt_tgt):
        e.wait_ge(s["zt"].h, w)
        e.tensor_add(b1p_col[:], pG[:, 0:32], bias_sb[:, 16:48])
        e.tensor_scalar_mul(b2q_col[:], bias_sb[:, 48:56],
                            0.25).then_inc(s["stats"].h, 1)
    P.on("vector", v_b1p)
    bias_ready = s["stats"].inc()

    w2q_done = [0] * 8
    for dt_ in range(8):
        qsl = qbuf[:]
        ssl = sclbuf[:]
        w_prev = s["deq"].v
        P.on("gpsimd", lambda e, w=w_prev: e.wait_ge(s["deq"].h, w))
        P.on("sync", lambda e, w=w_prev: e.wait_ge(s["deq"].h, w))
        def g_ld2(e, dt_=dt_, qsl=qsl, wp=s["gdma"].v):
            if wp:
                e.wait_ge(s["gdma"].h, wp)
            e.dma_start(out=qsl, in_=fc2_qc[dt_]).then_inc(s["gdma"].h, 16)
        P.on("gpsimd", g_ld2)
        def sy_ld2(e, dt_=dt_, ssl=ssl):
            e.dma_start(out=ssl, in_=fc2_scales_c[dt_]).then_inc(s["ld0"].h, 16)
        P.on("sync", sy_ld2)
        gq_tgt = s["gdma"].inc(16)
        ld_tgt = s["ld0"].inc(16)

        def v_deq2(e, qsl=qsl, ssl=ssl, w=ld_tgt, wg=gq_tgt):
            e.wait_ge(s["ld0"].h, w)
            e.wait_ge(s["gdma"].h, wg)
            e.scalar_tensor_tensor(out=wtmp[:], in0=qsl, scalar=-128.0,
                                   in1=ssl, op0=ALU.add, op1=ALU.mult)
            e.tensor_scalar_mul(wbf[:], wtmp[:], 0.25).then_inc(s["deq"].h, 1)
        P.on("vector", v_deq2)
        deq_tgt = s["deq"].inc()

        def sy_w2out(e, dt_=dt_, w=deq_tgt):
            e.wait_ge(s["deq"].h, w)
            e.dma_start(out=w2q_hbm[dt_], in_=wbf[:]).then_inc(s["ld0"].h, 16)
        P.on("sync", sy_w2out)
        wb_done = s["ld0"].inc(16)
        w2q_done[dt_] = wb_done
        P.on("vector", lambda e, w=wb_done: e.wait_ge(s["ld0"].h, w))

    if DEV_STAGE >= 3:
        TCUT = int(os.environ.get("MK_TCUT", "4"))
        GCUT = int(os.environ.get("MK_GCUT", "5"))
        TWO_PI = float(np.float32(2 * np.pi))
        INV_2PI = float(np.float32(1.0 / (2 * np.pi)))
        HALF_PI = float(np.float32(np.pi / 2))
        PI32 = float(np.float32(np.pi))
        GR = qbuf[:]
        DT, DSK, MRE, MIM, TAa, TBb, TCc = 0, 8, 16, 32, 48, 64, 80
        E1, AIMG, LRE, LIM, CREo, CIMo = 112, 368, 624, 880, 1136, 1392
        T1o, T2o, T3o, T4o, T5o, T6o = 1648, 1904, 2160, 2416, 2672, 2928
        LQRE, LQIM = 3184, 3440
        DTE, DT2 = 3696, 3704
        LTRE, LTIM, CT2RE, CT2IM = 0, 256, 512, 768
        CTTRE, CTTIM = 1024, 1280
        MTRE, MTIM, MTA, MTB, MTC = 1536, 1568, 1600, 1632, 1664
        LQTRE, LQTIM = 3136, 3392
        scl = sclbuf[:]
        scl_bf = scl.bitcast(BF16)
        kpad = scl_bf[:, 3584:3840]
        ktst = scl_bf[:, 3840:4096]
        wstg1 = scl_bf[:, 4224:5248]
        wstg2 = scl_bf[:, 5248:6272]
        h2f = h2_fm[:].bitcast(F32)
        kb1o, kb2o = 4096, 6144
        wgstage = h2_fm[:, 0:8192]
        PH = [wtmp[:], xhalf[:]]
        wcstage = xs_fm[:, 0:4096]

        gq = lambda off, n=256: GR[:, off:off + n]
        gs = lambda off, n=256: scl[:, off:off + n]

        def VI2(ins):
            s["vch"].inc()
            ins.then_inc(s["vch"].h, 1)

        def VW2(e):
            e.wait_ge(s["vch"].h, s["vch"].v)

        def ap3(base, off, fdims):
            return bass.AP(base.tensor, base.offset + off,
                           [[base.ap[0][0], 128]] + fdims)

        def sy_grids(e, w=s["deq"].v):
            e.wait_ge(s["deq"].h, w)
            e.dma_start(out=GR[:, DT:DT + 8], in_=dt_col).then_inc(s["ld0"].h, 16)
            e.dma_start(out=GR[:, DSK:DSK + 8], in_=dsk_col).then_inc(s["ld0"].h, 16)
            e.dma_start(out=gq(T1o), in_=lar_g).then_inc(s["ld0"].h, 16)
            e.dma_start(out=gq(AIMG), in_=aim_g).then_inc(s["ld0"].h, 16)
            e.dma_start(out=gq(CREo), in_=c_re_g).then_inc(s["ld0"].h, 16)
            e.dma_start(out=gq(CIMo), in_=c_im_g).then_inc(s["ld0"].h, 16)
        P.on("sync", sy_grids)
        grid_ld = s["ld0"].inc(96)

        def sc_g1(e, w=grid_ld):
            e.wait_ge(s["ld0"].h, w)
            e.activation(out=GR[:, DTE:DTE + 8], in_=GR[:, DT:DT + 8], func=AF.Exp)
            e.activation(out=gq(E1), in_=gq(T1o), func=AF.Exp).then_inc(s["sq"].h, 1)
        P.on("scalar", sc_g1)
        g1_tgt = s["sq"].inc()

        def v_g1(e, w=g1_tgt):
            e.wait_ge(s["sq"].h, w)
            VI2(e.tensor_scalar_mul(GR[:, DT2:DT2 + 8], GR[:, DTE:DTE + 8], INV_2PI))
            VW2(e)
            dtb = ap3(GR, DTE, [[1, 8], [0, 32]])
            dt2b = ap3(GR, DT2, [[1, 8], [0, 32]])
            aim3 = ap3(GR, AIMG, [[32, 8], [1, 32]])
            e13 = ap3(GR, E1, [[32, 8], [1, 32]])
            VI2(e.tensor_mul(ap3(GR, T1o, [[32, 8], [1, 32]]), aim3, dtb))
            VI2(e.tensor_mul(ap3(GR, T2o, [[32, 8], [1, 32]]), e13, dtb))
            VI2(e.tensor_mul(ap3(GR, T3o, [[32, 8], [1, 32]]), aim3, dt2b))
            VW2(e)
            VI2(e.tensor_scalar_add(gq(T4o), gq(T3o), 0.25))
            VW2(e)
            VI2(e.tensor_copy(gq(T5o).bitcast(I32), gq(T3o)))
            VI2(e.tensor_copy(gq(T6o).bitcast(I32), gq(T4o)))
            VW2(e)
            VI2(e.tensor_copy(gq(T3o), gq(T5o).bitcast(I32)))
            VI2(e.tensor_copy(gq(T4o), gq(T6o).bitcast(I32)))
            VW2(e)
            VI2(e.scalar_tensor_tensor(out=gq(T5o), in0=gq(T3o), scalar=-TWO_PI,
                                       in1=gq(T1o), op0=ALU.mult, op1=ALU.add))
            VI2(e.scalar_tensor_tensor(out=gq(T6o), in0=gq(T4o), scalar=-TWO_PI,
                                       in1=gq(T1o), op0=ALU.mult, op1=ALU.add))
            VW2(e)
            VI2(e.tensor_scalar_add(gq(T6o), gq(T6o), HALF_PI))
            VW2(e)
            for off in (T5o, T6o):
                VI2(e.tensor_scalar(out=gq(T3o), in0=gq(off), scalar1=PI32,
                                    scalar2=None, op0=ALU.is_gt))
                VI2(e.tensor_scalar(out=gq(T4o), in0=gq(off), scalar1=-PI32,
                                    scalar2=None, op0=ALU.is_lt))
                VW2(e)
                VI2(e.tensor_sub(gq(T4o), gq(T4o), gq(T3o)))
                VW2(e)
                ins = e.scalar_tensor_tensor(out=gq(off), in0=gq(T4o), scalar=TWO_PI,
                                             in1=gq(off), op0=ALU.mult, op1=ALU.add)
                if off == T6o:
                    ins.then_inc(s["stats"].h, 1)
                else:
                    VI2(ins)
                    VW2(e)
        P.on("vector", v_g1)
        vg1_tgt = s["stats"].inc()

        def sc_g2(e, w=vg1_tgt):
            e.wait_ge(s["stats"].h, w)
            e.activation(out=gq(T1o), in_=gq(T2o), func=AF.Exp, scale=-1.0)
            load_act(e, "silu_and_others")
            e.activation(out=gq(T3o), in_=gq(T5o), func=AF.Sin)
            e.activation(out=gq(T4o), in_=gq(T6o), func=AF.Sin).then_inc(s["sq"].h, 1)
        P.on("scalar", sc_g2)
        g2_tgt = s["sq"].inc()

        def v_g2(e, w=g2_tgt):
            e.wait_ge(s["sq"].h, w)
            VI2(e.tensor_mul(gq(LIM), gq(T1o), gq(T3o)))
            VI2(e.tensor_mul(gq(LRE), gq(T1o), gq(T4o)))
            VW2(e)
            VI2(e.tensor_mul(gq(T3o), gq(E1), gq(E1)))
            VI2(e.tensor_mul(gq(T4o), gq(AIMG), gq(AIMG)))
            VW2(e)
            VI2(e.tensor_add(gq(T3o), gq(T3o), gq(T4o)))
            VW2(e)
            VI2(e.reciprocal(gq(T3o), gq(T3o)))
            VI2(e.tensor_scalar_add(gq(T4o), gq(LRE), -1.0))
            VW2(e)
            VI2(e.tensor_mul(gq(T5o), gq(T4o), gq(E1)))
            VI2(e.tensor_mul(gq(T6o), gq(LIM), gq(AIMG)))
            VW2(e)
            VI2(e.tensor_sub(gq(T5o), gq(T6o), gq(T5o)))
            VW2(e)
            VI2(e.tensor_mul(gq(T5o), gq(T5o), gq(T3o)))
            VI2(e.tensor_mul(gq(T6o), gq(LIM), gq(E1)))
            VI2(e.tensor_mul(gq(T1o), gq(T4o), gq(AIMG)))
            VW2(e)
            VI2(e.tensor_add(gq(T6o), gq(T6o), gq(T1o)))
            VW2(e)
            VI2(e.tensor_mul(gq(T6o), gq(T6o), gq(T3o)))
            VW2(e)
            VI2(e.tensor_scalar_mul(gq(T6o), gq(T6o), -1.0))
            VW2(e)
            VI2(e.tensor_mul(gq(T1o), gq(CREo), gq(T5o)))
            VI2(e.tensor_mul(gq(T2o), gq(CIMo), gq(T6o)))
            VI2(e.tensor_mul(gq(T3o), gq(CREo), gq(T6o)))
            VI2(e.tensor_mul(gq(T4o), gq(CIMo), gq(T5o)))
            VW2(e)
            VI2(e.tensor_sub(gq(T1o), gq(T1o), gq(T2o)))
            VI2(e.tensor_add(gq(T3o), gq(T3o), gq(T4o)))
            VW2(e)
            VI2(e.tensor_copy(gq(CREo), gq(T1o)))
            VI2(e.tensor_copy(gq(CIMo), gq(T3o)))
            VW2(e)
            VI2(e.tensor_mul(gq(T2o), gq(CREo), gq(LRE)))
            VI2(e.tensor_mul(gq(T4o), gq(CIMo), gq(LIM)))
            VI2(e.tensor_mul(gq(T5o), gq(CREo), gq(LIM)))
            VI2(e.tensor_mul(gq(T6o), gq(CIMo), gq(LRE)))
            VW2(e)
            VI2(e.tensor_sub(gq(T2o), gq(T2o), gq(T4o)))
            VI2(e.tensor_add(gq(T5o), gq(T5o), gq(T6o)))
            VW2(e)
            VI2(e.tensor_scalar_mul(gs(CT2RE), gq(T2o), 2.0))
            VI2(e.tensor_scalar_mul(gs(CT2IM), gq(T5o), 2.0))
            VI2(e.tensor_scalar_mul(gq(CREo), gq(CREo), 2.0))
            e.tensor_scalar_mul(gq(CIMo), gq(CIMo), -2.0).then_inc(s["stats"].h, 1)
        P.on("vector", v_g2)
        grids_done = s["stats"].inc()

        kt_store_tgt = 0
        bank_last_evac = {}
        for g in range(NG if TCUT >= 2 else 0):
            ph_tgts = []
            for hf in range(2):
                Pb = PH[hf]
                def v_pb(e, g=g, hf=hf, Pb=Pb, w=grids_done, w2=w2q_done[7]):
                    if g == 0 and hf == 0:
                        e.wait_ge(s["stats"].h, w)
                        e.wait_ge(s["ld0"].h, w2)
                    e.memset(ap3(Pb, 0, [[256, 16], [1, 1]]), 1.0)
                    e.memset(ap3(Pb, 128, [[256, 16], [1, 1]]), 0.0)
                    co = g * 32 + 16 * hf
                    VI2(e.tensor_copy(GR[:, MRE:MRE + 16], GR[:, LRE + co:LRE + co + 16]))
                    VI2(e.tensor_copy(GR[:, MIM:MIM + 16], GR[:, LIM + co:LIM + co + 16]))
                    VW2(e)
                    for k in range(7):
                        cnt = 1 << k
                        src_re = ap3(Pb, 0, [[256, 16], [1, cnt]])
                        src_im = ap3(Pb, 128, [[256, 16], [1, cnt]])
                        dst_re = ap3(Pb, cnt, [[256, 16], [1, cnt]])
                        dst_im = ap3(Pb, 128 + cnt, [[256, 16], [1, cnt]])
                        mre = ap3(GR, MRE, [[1, 16], [0, cnt]])
                        mim = ap3(GR, MIM, [[1, 16], [0, cnt]])
                        t1 = ap3(h2f, kb1o, [[cnt, 16], [1, cnt]])
                        t2 = ap3(h2f, kb2o, [[cnt, 16], [1, cnt]])
                        VI2(e.tensor_mul(t1, src_re, mim))
                        VI2(e.tensor_mul(t2, src_im, mre))
                        VW2(e)
                        VI2(e.tensor_add(dst_im, t1, t2))
                        VI2(e.tensor_mul(t1, src_re, mre))
                        VI2(e.tensor_mul(t2, src_im, mim))
                        VW2(e)
                        VI2(e.tensor_sub(dst_re, t1, t2))
                        VW2(e)
                        VI2(e.tensor_mul(GR[:, TAa:TAa + 16], GR[:, MRE:MRE + 16], GR[:, MIM:MIM + 16]))
                        VI2(e.tensor_mul(GR[:, TBb:TBb + 16], GR[:, MRE:MRE + 16], GR[:, MRE:MRE + 16]))
                        VI2(e.tensor_mul(GR[:, TCc:TCc + 16], GR[:, MIM:MIM + 16], GR[:, MIM:MIM + 16]))
                        VW2(e)
                        VI2(e.tensor_sub(GR[:, MRE:MRE + 16], GR[:, TBb:TBb + 16], GR[:, TCc:TCc + 16]))
                        VI2(e.tensor_scalar_mul(GR[:, MIM:MIM + 16], GR[:, TAa:TAa + 16], 2.0))
                        VW2(e)
                    VI2(e.tensor_copy(GR[:, LQRE + co:LQRE + co + 16], GR[:, MRE:MRE + 16]))
                    e.tensor_copy(GR[:, LQIM + co:LQIM + co + 16],
                                  GR[:, MIM:MIM + 16]).then_inc(s["w1"].h, 1)
                P.on("vector", v_pb)
                ph_tgts.append(s["w1"].inc())

            for hf in range(2):
                Pb = PH[hf]
                ph_tgt = ph_tgts[hf]
                for rnd in range(8 if GCUT >= 2 else 0):
                    bank = pF if rnd % 2 == 0 else pG
                    bkey = "pF" if rnd % 2 == 0 else "pG"
                    bw_prev = bank_last_evac.get(bkey, 0)
                    def t_wgt(e, Pb=Pb, rnd=rnd, bank=bank, w=ph_tgt, bp=bw_prev):
                        if bp > 0:
                            e.wait_ge(s["ztfree"].h, bp)
                        e.wait_ge(s["w1"].h, w)
                        for j in range(4):
                            pl = rnd * 4 + j
                            ins = e.transpose(bank[:, j * 128:(j + 1) * 128],
                                              Pb[:, pl * 128:(pl + 1) * 128],
                                              ident32[:])
                        ins.then_inc(s["zt"].h, 1)
                    P.on("tensor", t_wgt)
                    zt_tgt = s["zt"].inc()

                    def v_wge(e, hf=hf, rnd=rnd, bank=bank, w=zt_tgt):
                        e.wait_ge(s["zt"].h, w)
                        for j in range(4):
                            pl = rnd * 4 + j
                            nl, r = pl // 2, pl % 2
                            nr = r * 32 + (16 * hf + nl)
                            out_ap = ap3(wgstage, nr * 64, [[4096, 2], [1, 64]])
                            in_ap = ap3(bank[:], j * 128, [[64, 2], [1, 64]])
                            ins = e.tensor_copy(out_ap, in_ap)
                        ins.then_inc(s["ztfree"].h, 1)
                    P.on("vector", v_wge)
                    bank_last_evac[bkey] = s["ztfree"].inc()

            wg_ev_done = s["ztfree"].v

            def v_k(e, g=g, w=wg_ev_done, skip=(GCUT < 4)):
                if skip:
                    return
                e.wait_ge(s["ztfree"].h, w)
                kb1a = ap3(h2f, kb1o, [[128, 16], [1, 128]])
                kb2a = ap3(h2f, kb2o, [[128, 16], [1, 128]])
                first = True
                for hf in range(2):
                    Pb = PH[hf]
                    co = g * 32 + 16 * hf
                    pre = ap3(Pb, 0, [[256, 16], [1, 128]])
                    pim = ap3(Pb, 128, [[256, 16], [1, 128]])
                    creb = ap3(GR, CREo + co, [[1, 16], [0, 128]])
                    cimb = ap3(GR, CIMo + co, [[1, 16], [0, 128]])
                    for src, cb in ((pre, creb), (pim, cimb)):
                        if first:
                            VI2(e.tensor_mul(kb1a, src, cb))
                            first = False
                        else:
                            VI2(e.tensor_mul(kb2a, src, cb))
                            VW2(e)
                            VI2(e.tensor_add(kb1a, kb1a, kb2a))
                        VW2(e)
                for width in (1024, 512, 256, 128):
                    VI2(e.tensor_add(h2f[:, kb1o:kb1o + width],
                                     h2f[:, kb1o:kb1o + width],
                                     h2f[:, kb1o + width:kb1o + 2 * width]))
                    VW2(e)
                VI2(e.tensor_add(h2f[:, kb1o:kb1o + 1], h2f[:, kb1o:kb1o + 1],
                                 GR[:, DSK + g:DSK + g + 1]))
                e.memset(kpad[:, 0:128], 0.0)
                VW2(e)
                e.tensor_copy(kpad[:, 128:256],
                              h2f[:, kb1o:kb1o + 128]).then_inc(s["bw"].h, 1)
            P.on("vector", v_k)
            k_tgt = s["bw"].inc() if GCUT >= 4 else s["bw"].v

            def t_kt(e, w=k_tgt, skip=(GCUT < 4)):
                if skip:
                    return
                e.wait_ge(s["bw"].h, w)
                e.transpose(pE[:, 0:128], kpad[:, 0:128], ident[:])
                e.transpose(pE[:, 128:256], kpad[:, 128:256],
                            ident[:]).then_inc(s["zt"].h, 1)
            P.on("tensor", t_kt)
            kt_tgt = s["zt"].inc() if GCUT >= 4 else s["zt"].v

            def v_kte(e, w=kt_tgt, wst=kt_store_tgt, skip=(GCUT < 4)):
                if skip:
                    return
                e.wait_ge(s["zt"].h, w)
                if wst > 0:
                    e.wait_ge(s["ld0"].h, wst)
                e.tensor_copy(ktst[:, 0:128], pE[:, 0:128])
                e.tensor_copy(ktst[:, 128:256], pE[:, 128:256]).then_inc(s["ztfree"].h, 1)
            P.on("vector", v_kte)
            kte_tgt = s["ztfree"].inc() if GCUT >= 4 else s["ztfree"].v

            def sy_kt(e, g=g, w=kte_tgt, skip=(GCUT < 4)):
                if skip:
                    return
                e.wait_ge(s["ztfree"].h, w)
                for tile in range(2):
                    for qq in range(4):
                        e.dma_start(out=kT_hbm[4 * g + qq, tile * 128:(tile + 1) * 128, :],
                                    in_=ktst[:, tile * 128 + qq * 32: tile * 128 + qq * 32 + 32]
                                    ).then_inc(s["ld0"].h, 16)
            P.on("sync", sy_kt)
            if GCUT >= 4:
                kt_store_tgt = s["ld0"].inc(128)

            if GCUT >= 3:
                def sy_wg(e, g=g, w=wg_ev_done):
                    e.wait_ge(s["ztfree"].h, w)
                    e.dma_start(out=wg_hbm[g], in_=wgstage).then_inc(s["ld0"].h, 16)
                P.on("sync", sy_wg)
                wg_st = s["ld0"].inc(16)
                P.on("vector", lambda e, w=wg_st: e.wait_ge(s["ld0"].h, w))

            grids4 = [(LRE, LTRE, GR), (LIM, LTIM, GR),
                      (CT2RE, CTTRE, scl), (CT2IM, CTTIM, scl),
                      (LQRE, LQTRE, GR), (LQIM, LQTIM, GR)]
            ltstage = h2f[:, kb1o:kb1o + 128]
            for srco, dsto, srcb in (grids4 if GCUT >= 5 else []):
                def v_lts(e, g=g, srco=srco, srcb=srcb):
                    src = bass.AP(srcb.tensor, srcb.offset + srco + g * 32,
                                  [[srcb.ap[0][0], 128], [0, 4], [1, 32]])
                    e.tensor_copy(ap3(h2f, kb1o, [[32, 4], [1, 32]]),
                                  src).then_inc(s["bw"].h, 1)
                P.on("vector", v_lts)
                lts_tgt = s["bw"].inc()

                def t_lt(e, w=lts_tgt, fr=s["ztfree"].v):
                    if fr > 0:
                        e.wait_ge(s["ztfree"].h, fr)
                    e.wait_ge(s["bw"].h, w)
                    e.transpose(pG[:, 0:128], ltstage,
                                ident32[:]).then_inc(s["zt"].h, 1)
                P.on("tensor", t_lt)
                lt_tgt = s["zt"].inc()

                def v_lte(e, g=g, dsto=dsto, w=lt_tgt):
                    e.wait_ge(s["zt"].h, w)
                    pga = pG[:]
                    for m in range(4):
                        src = bass.AP(pga.tensor, pga.offset + 32 * m * 512 + m,
                                      [[512, 32], [4, 32]])
                        ins = e.tensor_copy(
                            scl[32 * m:32 * m + 32, dsto + g * 32: dsto + g * 32 + 32], src)
                    ins.then_inc(s["ztfree"].h, 1)
                P.on("vector", v_lte)
                s["ztfree"].inc()

        def sy_lq(e, w=s["w1"].v, skip=(TCUT < 2)):
            if skip:
                return
            e.wait_ge(s["w1"].h, w)
            e.dma_start(out=lamq_hbm, in_=GR[:, LQRE:LQRE + 512]).then_inc(s["ld0"].h, 16)
        P.on("sync", sy_lq)
        if TCUT >= 2:
            s["ld0"].inc(16)

        for b in range(8 if TCUT >= 3 else 0):
            def v_wc(e, b=b, w=s["ztfree"].v):
                if b == 0:
                    e.wait_ge(s["ztfree"].h, w)
                e.memset(ap3(PH[0], 0, [[128, 32], [1, 1]]), 1.0)
                e.memset(ap3(PH[1], 0, [[128, 32], [1, 1]]), 0.0)
                VI2(e.tensor_copy(scl[:, MTRE:MTRE + 32],
                                  scl[:, LTRE + 32 * b: LTRE + 32 * b + 32]))
                VI2(e.tensor_copy(scl[:, MTIM:MTIM + 32],
                                  scl[:, LTIM + 32 * b: LTIM + 32 * b + 32]))
                VW2(e)
                for k in range(7):
                    cnt = 1 << k
                    sre = ap3(PH[0], 0, [[128, 32], [1, cnt]])
                    sim = ap3(PH[1], 0, [[128, 32], [1, cnt]])
                    dre = ap3(PH[0], cnt, [[128, 32], [1, cnt]])
                    dim = ap3(PH[1], cnt, [[128, 32], [1, cnt]])
                    mre = ap3(scl, MTRE, [[1, 32], [0, cnt]])
                    mim = ap3(scl, MTIM, [[1, 32], [0, cnt]])
                    t1 = ap3(h2f, kb1o, [[cnt, 32], [1, cnt]])
                    t2 = ap3(h2f, kb2o, [[cnt, 32], [1, cnt]])
                    VI2(e.tensor_mul(t1, sre, mim))
                    VI2(e.tensor_mul(t2, sim, mre))
                    VW2(e)
                    VI2(e.tensor_add(dim, t1, t2))
                    VI2(e.tensor_mul(t1, sre, mre))
                    VI2(e.tensor_mul(t2, sim, mim))
                    VW2(e)
                    if k == 6:
                        e.tensor_sub(dre, t1, t2).then_inc(s["bw"].h, 1)
                    else:
                        VI2(e.tensor_sub(dre, t1, t2))
                        VW2(e)
                        VI2(e.tensor_mul(scl[:, MTA:MTA + 32], scl[:, MTRE:MTRE + 32],
                                         scl[:, MTIM:MTIM + 32]))
                        VI2(e.tensor_mul(scl[:, MTB:MTB + 32], scl[:, MTRE:MTRE + 32],
                                         scl[:, MTRE:MTRE + 32]))
                        VI2(e.tensor_mul(scl[:, MTC:MTC + 32], scl[:, MTIM:MTIM + 32],
                                         scl[:, MTIM:MTIM + 32]))
                        VW2(e)
                        VI2(e.tensor_sub(scl[:, MTRE:MTRE + 32], scl[:, MTB:MTB + 32],
                                         scl[:, MTC:MTC + 32]))
                        VI2(e.tensor_scalar_mul(scl[:, MTIM:MTIM + 32],
                                                scl[:, MTA:MTA + 32], 2.0))
                        VW2(e)
            P.on("vector", v_wc)
            pp_tgt = s["bw"].inc()

            for comp in range(2):
                stc = s["ld0"].v
                def v_wcs(e, b=b, comp=comp, w=pp_tgt, ws=stc):
                    e.wait_ge(s["bw"].h, w)
                    if ws > 0:
                        e.wait_ge(s["ld0"].h, ws)
                    for phh in range(2):
                        po = phh * 64
                        ppre_s = ap3(PH[0], po, [[128, 32], [1, 64]])
                        ppim_s = ap3(PH[1], po, [[128, 32], [1, 64]])
                        ctre_b = ap3(scl, CTTRE + 32 * b, [[1, 32], [0, 64]])
                        ctim_b = ap3(scl, CTTIM + 32 * b, [[1, 32], [0, 64]])
                        t1 = ap3(h2f, kb1o, [[64, 32], [1, 64]])
                        t2 = ap3(h2f, kb2o, [[64, 32], [1, 64]])
                        out = ap3(wcstage, po, [[128, 32], [1, 64]])
                        last = (phh == 1)
                        if comp == 0:
                            VI2(e.tensor_mul(t1, ppre_s, ctre_b))
                            VI2(e.tensor_mul(t2, ppim_s, ctim_b))
                            VW2(e)
                            if last:
                                e.tensor_sub(out, t1, t2).then_inc(s["bwc"].h, 1)
                            else:
                                VI2(e.tensor_sub(out, t1, t2))
                        else:
                            VI2(e.tensor_mul(t1, ppim_s, ctre_b))
                            VI2(e.tensor_mul(t2, ppre_s, ctim_b))
                            VW2(e)
                            VI2(e.tensor_add(t1, t1, t2))
                            VW2(e)
                            if last:
                                e.tensor_scalar_mul(out, t1, -1.0).then_inc(s["bwc"].h, 1)
                            else:
                                VI2(e.tensor_scalar_mul(out, t1, -1.0))
                P.on("vector", v_wcs)
                wcs_tgt = s["bwc"].inc()

                def sy_wcs(e, b=b, comp=comp, w=wcs_tgt):
                    e.wait_ge(s["bwc"].h, w)
                    dst = wcre_hbm if comp == 0 else wcim_hbm
                    e.dma_start(out=dst[:, b * 4096:(b + 1) * 4096],
                                in_=wcstage).then_inc(s["ld0"].h, 16)
                P.on("sync", sy_wcs)
                s["ld0"].inc(16)

        wlist = [(glu_W, gluWT_hbm, 16, 2048), (out_W, outWT_hbm, 8, 1024)]
        if TCUT < 4:
            wlist = []
        prev_cast_tgt = 0
        prev_store_tgt = 0
        for W_in, W_out, n_ot, ocols in wlist:
            for ot in range(n_ot):
                def sy_wl(e, W_in=W_in, ot=ot, w=s["bwc"].v, wc=prev_cast_tgt):
                    e.wait_ge(s["bwc"].h, w)
                    if wc > 0:
                        e.wait_ge(s["bw"].h, wc)
                    e.dma_start(out=wtmp[:, 0:1024],
                                in_=W_in[ot * 128:(ot + 1) * 128, :]).then_inc(s["ld0"].h, 16)
                P.on("sync", sy_wl)
                wl_tgt = s["ld0"].inc(16)

                def v_wcast(e, w=wl_tgt, wt=s["zt"].v):
                    e.wait_ge(s["ld0"].h, w)
                    if wt > 0:
                        e.wait_ge(s["zt"].h, wt)
                    e.tensor_copy(wstg1, wtmp[:, 0:1024]).then_inc(s["bw"].h, 1)
                P.on("vector", v_wcast)
                prev_cast_tgt = s["bw"].inc()

                def t_wtr(e, w=prev_cast_tgt, fr=s["ztfree"].v):
                    e.wait_ge(s["bw"].h, w)
                    if fr > 0:
                        e.wait_ge(s["ztfree"].h, fr)
                    for kt in range(8):
                        ins = e.transpose(pE[:, kt * 128:(kt + 1) * 128],
                                          wstg1[:, kt * 128:(kt + 1) * 128], ident[:])
                    ins.then_inc(s["zt"].h, 1)
                P.on("tensor", t_wtr)
                wt_tgt = s["zt"].inc()

                def v_wev(e, w=wt_tgt, ws=prev_store_tgt):
                    e.wait_ge(s["zt"].h, w)
                    if ws > 0:
                        e.wait_ge(s["ld0"].h, ws)
                    for kt in range(8):
                        ins = e.tensor_copy(wstg2[:, kt * 128:(kt + 1) * 128],
                                            pE[:, kt * 128:(kt + 1) * 128])
                    ins.then_inc(s["ztfree"].h, 1)
                P.on("vector", v_wev)
                we_tgt = s["ztfree"].inc()

                def sy_wst(e, W_out=W_out, ot=ot, w=we_tgt):
                    e.wait_ge(s["ztfree"].h, w)
                    e.dma_start(out=W_out[:, ot * 1024:(ot + 1) * 1024],
                                in_=wstg2).then_inc(s["ld0"].h, 16)
                P.on("sync", sy_wst)
                prev_store_tgt = s["ld0"].inc(16)
        tables_done = ("ld0", s["ld0"].v)
    else:
        tables_done = ("deq", 16)

    def emit_ffn(x_src, x_dst, w1g_gate):
        gate_sem, gate_val = w1g_gate
        SU, SQ, M, MSQ, M2, VAR, VEPS, RCP, VR, LNV, LNW, R1, R2, SS, CC, S2, C2, TT = \
            (0, 4, 8, 12, 16, 20, 24, 28, 32, 36, 40, 44, 48, 52, 56, 60, 64, 68)
        g4 = lambda off: statrow[:, off:off + 4]
        g1 = lambda off, i: statrow[:, off + i:off + i + 1]

        def VI(ins):
            s["vch"].inc()
            ins.then_inc(s["vch"].h, 1)

        def VW(e):
            e.wait_ge(s["vch"].h, s["vch"].v)

        for ch in range(8):
            xchunk = wtmp
            xfree_need = s["xfree"].v

            def sy_x(e, ch=ch, wf=xfree_need, first=(ch == 0),
                     gs=gate_sem, gv=gate_val):
                if first:
                    e.wait_ge(s[gs].h, gv)
                if wf > 0:
                    e.wait_ge(s["xfree"].h, wf)
                for lt4 in range(4):
                    lt = ch * 4 + lt4
                    e.dma_start(out=xchunk[:, lt4 * D:(lt4 + 1) * D],
                                in_=x_src[lt * 128:(lt + 1) * 128, :]).then_inc(s["xld"].h, 16)
            P.on("sync", sy_x)
            xld_tgt = s["xld"].inc(64)

            xst_prev = s["xst"].v
            def sc_sq(e, w=xld_tgt, bw=bias_ready, wx=xst_prev, ch=ch,
                      gs=gate_sem, gv=gate_val):
                if ch == 0:
                    e.wait_ge(s[gs].h, gv)
                load_act(e, "natural_log_exp_and_others")
                e.wait_ge(s["stats"].h, bw)
                e.wait_ge(s["xld"].h, w)
                if wx > 0:
                    e.wait_ge(s["xst"].h, wx)
                for lt4 in range(4):
                    ins = e.activation(out=xsbuf[:], in_=xchunk[:, lt4 * D:(lt4 + 1) * D],
                                       func=AF.Square, accum_out=g1(SQ, lt4))
                ins.then_inc(s["sq"].h, 1)
            P.on("scalar", sc_sq)
            sq_tgt = s["sq"].inc()

            def v_chain(e, w=xld_tgt, wsq=sq_tgt):
                e.wait_ge(s["xld"].h, w)
                for lt4 in range(4):
                    ins = e.reduce_sum(g1(SU, lt4), xchunk[:, lt4 * D:(lt4 + 1) * D], axis=AX.X)
                VI(ins)
                e.wait_ge(s["sq"].h, wsq)
                VW(e)
                VI(e.tensor_scalar_mul(g4(M), g4(SU), 1.0 / D))
                VI(e.tensor_scalar_mul(g4(MSQ), g4(SQ), 1.0 / D))
                VW(e)
                VI(e.tensor_mul(g4(M2), g4(M), g4(M)))
                VW(e)
                VI(e.tensor_sub(g4(VAR), g4(MSQ), g4(M2)))
                VW(e)
                VI(e.tensor_scalar_add(g4(VEPS), g4(VAR), EPS))
                VW(e)
                VI(e.reciprocal(g4(RCP), g4(VEPS)))
                VW(e)
                VI(e.tensor_mul(g4(VR), g4(VAR), g4(RCP)))
                VW(e)
                e.tensor_scalar_add(g4(VR), g4(VR), EPS).then_inc(s["stats"].h, 1)
            P.on("vector", v_chain)
            st_tgt = s["stats"].inc()

            ln_tgt = s["sq"].inc()
            rsq_tgt = s["sq"].inc()

            def sc_rsq(e, w=st_tgt, wl=ln_tgt):
                e.wait_ge(s["stats"].h, w)
                e.activation(out=g4(LNV), in_=g4(VEPS), func=AF.Ln)
                e.activation(out=g4(LNW), in_=g4(VR), func=AF.Ln).then_inc(s["sq"].h, 1)
                e.wait_ge(s["sq"].h, wl)
                e.activation(out=g4(R1), in_=g4(LNV), func=AF.Exp, scale=-0.5)
                e.activation(out=g4(R2), in_=g4(LNW), func=AF.Exp,
                             scale=-0.5).then_inc(s["sq"].h, 1)
            P.on("scalar", sc_rsq)

            def v_chain2(e, w=rsq_tgt):
                e.wait_ge(s["sq"].h, w)
                VI(e.tensor_mul(g4(SS), g4(R1), g4(R2)))
                e.tensor_scalar(out=g4(S2), in0=g4(R1), scalar1=0.5, scalar2=1.0,
                                op0=ALU.mult, op1=ALU.add)
                VI(e.tensor_mul(g4(TT), g4(M), g4(R1)))
                VW(e)
                e.scalar_tensor_tensor(out=g4(CC), in0=g4(M), scalar=-1.0,
                                       in1=g4(SS), op0=ALU.mult, op1=ALU.mult)
                e.tensor_scalar_mul(g4(C2), g4(TT), -0.5).then_inc(s["stats"].h, 1)
            P.on("vector", v_chain2)
            st_tgt = s["stats"].inc()

            for lt4 in range(4):
                xst_prev2 = s["xst"].v

                def sc_apply(e, lt4=lt4, w=st_tgt, wx=xst_prev2):
                    e.wait_ge(s["stats"].h, w)
                    if wx > 0:
                        e.wait_ge(s["xst"].h, wx)
                    e.activation(out=xsbuf[:], in_=xchunk[:, lt4 * D:(lt4 + 1) * D],
                                 func=AF.Identity,
                                 scale=g1(SS, lt4), bias=g1(CC, lt4)).then_inc(s["xs"].h, 1)
                    e.activation(out=xhalf[:, lt4 * D:(lt4 + 1) * D],
                                 in_=xchunk[:, lt4 * D:(lt4 + 1) * D],
                                 func=AF.Identity,
                                 scale=g1(S2, lt4), bias=g1(C2, lt4)).then_inc(s["xfree"].h, 1)
                P.on("scalar", sc_apply)
                xs_tgt = s["xs"].inc()
                s["xfree"].inc()

                def t_xst(e, w=xs_tgt, fr=s["xsfm"].v,
                          first=(ch == 0 and lt4 == 0)):
                    if first:
                        e.wait_ge(s["gld"].h, ident_ready)
                    e.wait_ge(s["xs"].h, w)
                    if fr > 0:
                        e.wait_ge(s["xsfm"].h, fr)
                    for kt in range(8):
                        ins = e.transpose(pE[:, kt * 128:(kt + 1) * 128],
                                          xsbuf[:, kt * 128:(kt + 1) * 128], ident[:])
                    ins.then_inc(s["xst"].h, 1)
                P.on("tensor", t_xst)
                xst_tgt = s["xst"].inc()

                def v_xsfm(e, w=xst_tgt, lt4=lt4):
                    e.wait_ge(s["xst"].h, w)
                    for kt in range(8):
                        ins = e.tensor_copy(
                            xs_fm[:, kt * 512 + lt4 * 128: kt * 512 + (lt4 + 1) * 128],
                            pE[:, kt * 128:(kt + 1) * 128])
                    ins.then_inc(s["xsfm"].h, 1)
                P.on("vector", v_xsfm)
                s["xsfm"].inc()

            xsfm_need = s["xsfm"].v
            for ft in range(32):
                bank = pA if ft % 2 == 0 else pB
                h2_before = s["h2"].v
                def t_fc1(e, ft=ft, bank=bank, w=xsfm_need, hn=h2_before,
                          ch=ch, gs=gate_sem, gv=gate_val):
                    if ft == 0:
                        if ch == 0:
                            e.wait_ge(s[gs].h, gv)
                        e.wait_ge(s["xsfm"].h, w)
                    if ft >= 2:
                        e.wait_ge(s["h2"].h, hn - 1)
                    for kt in range(8):
                        col = kt * FF + ft * 128
                        ins = e.matmul(bank[:], w1g[:, col:col + 128],
                                       xs_fm[:, kt * 512:(kt + 1) * 512],
                                       start=(kt == 0), stop=(kt == 7))
                    ins.then_inc(s["fc1"].h, 1)
                P.on("tensor", t_fc1)
                fc1_tgt = s["fc1"].inc()

                def sc_silu(e, ft=ft, bank=bank, w=fc1_tgt, ch=ch,
                            wq=w2q_done[7] if w2q_done[7] else 0):
                    if ft == 0:
                        load_act(e, "silu_and_others")
                    if ft == 0 and ch == 0:
                        if wq:
                            e.wait_ge(s["ld0"].h, wq)
                    e.wait_ge(s["fc1"].h, w)
                    e.activation(out=h2_fm[:, ft * 512:(ft + 1) * 512], in_=bank[:],
                                 func=AF.Silu,
                                 bias=b1p_col[:, ft:ft + 1]).then_inc(s["h2"].h, 1)
                P.on("scalar", sc_silu)
                s["h2"].inc()

            h2_need = s["h2"].v
            for dt_ in range(8):
                wb = w2buf[:, (dt_ % 2) * 4096:(dt_ % 2 + 1) * 4096]
                w2f_before = s["w2free"].v
                if w2f_before >= 1:
                    P.on("sync", lambda e, w=w2f_before - 1: (
                        e.wait_ge(s["w2free"].h, w) if w > 0 else None))
                def sy_w2(e, dt_=dt_, wb=wb, ch=ch, wd=w2q_done[dt_]):
                    if ch == 0:
                        e.wait_ge(s["ld0"].h, wd)
                    e.dma_start(out=wb, in_=w2q_hbm[dt_]).then_inc(s["w2ld"].h, 16)
                P.on("sync", sy_w2)
                w2_tgt = s["w2ld"].inc(16)

                bank = pC if dt_ % 2 == 0 else pD
                zev_before = s["zev"].v
                def t_fc2(e, wb=wb, bank=bank, w=w2_tgt, hn=h2_need, dt_=dt_, zn=zev_before):
                    e.wait_ge(s["w2ld"].h, w)
                    if dt_ == 0:
                        e.wait_ge(s["h2"].h, hn)
                    if dt_ >= 2:
                        e.wait_ge(s["zev"].h, zn - 1)
                    for ft in range(32):
                        ins = e.matmul(bank[:], wb[:, ft * 128:(ft + 1) * 128],
                                       h2_fm[:, ft * 512:(ft + 1) * 512],
                                       start=(ft == 0), stop=(ft == 31))
                    ins.then_inc(s["fc2"].h, 1)
                P.on("tensor", t_fc2)
                fc2_tgt = s["fc2"].inc()

                def sc_zev(e, dt_=dt_, bank=bank, w=fc2_tgt):
                    e.wait_ge(s["fc2"].h, w)
                    e.activation(out=zfm[:], in_=bank[:], func=AF.Identity,
                                 bias=b2q_col[:, dt_:dt_ + 1]).then_inc(s["zev"].h, 1)
                    e.activation(out=statrow[:, 23:24], in_=statrow[:, 2:3],
                                 func=AF.Identity).then_inc(s["w2free"].h, 1)
                P.on("scalar", sc_zev)
                zev_tgt = s["zev"].inc()
                s["w2free"].inc()

                ztf_before = s["ztfree"].v
                def t_zt(e, w=zev_tgt, fr=ztf_before):
                    e.wait_ge(s["zev"].h, w)
                    if fr > 0:
                        e.wait_ge(s["ztfree"].h, fr)
                    for lt4 in range(4):
                        ins = e.transpose(pF[:, lt4 * 128:(lt4 + 1) * 128],
                                          zfm[:, lt4 * 128:(lt4 + 1) * 128], ident32[:])
                    ins.then_inc(s["zt"].h, 1)
                P.on("tensor", t_zt)
                zt_tgt = s["zt"].inc()

                def v_x1(e, dt_=dt_, w=zt_tgt):
                    e.wait_ge(s["zt"].h, w)
                    for lt4 in range(4):
                        ins = e.tensor_add(
                            x1part[lt4][:, dt_ * 128:(dt_ + 1) * 128],
                            pF[:, lt4 * 128:(lt4 + 1) * 128],
                            xhalf[:, lt4 * D + dt_ * 128: lt4 * D + (dt_ + 1) * 128])
                    ins.then_inc(s["ztfree"].h, 1)
                    if dt_ == 7:
                        e.tensor_copy(statrow[:, 22:23],
                                      statrow[:, 2:3]).then_inc(s["x1rdy"].h, 1)
                P.on("vector", v_x1)
                s["ztfree"].inc()
            x1r_tgt = s["x1rdy"].inc()

            def sy_x1(e, ch=ch, w=x1r_tgt):
                e.wait_ge(s["x1rdy"].h, w)
                for lt4 in range(4):
                    lt = ch * 4 + lt4
                    e.dma_start(out=x_dst[lt * 128:(lt + 1) * 128, :],
                                in_=x1part[lt4][:]).then_inc(s["x1st"].h, 16)
            P.on("sync", sy_x1)
            x1st_tgt = s["x1st"].inc(64)
            P.on("scalar", lambda e, w=x1st_tgt: e.wait_ge(s["x1st"].h, w))
            P.on("vector", lambda e, w=x1st_tgt: e.wait_ge(s["x1st"].h, w))

    emit_ffn(x_in, x1_hbm, tables_done)

    if DEV_STAGE >= 2:
        barrier(pre_waits=[("sync", "x1st", s["x1st"].v),
                           ("gpsimd", "gdma", s["gdma"].v),
                           ("tensor", "ztfree", s["ztfree"].v)])

        P2CUT = int(os.environ.get("MK_P2CUT", "3"))
        u_T = w1g
        in_WT = h2_fm[:, 8192:16384]
        in_bB_sb = wtmp[:, 0:1024]
        x1chunk = qbuf
        t2fm = [xs_fm[:, 0:1024], xs_fm[:, 1024:2048]]
        winw = wtmp[:, 1024:2048]
        winwb = zfm[:].bitcast(BF16)

        def sy_inb(e):
            e.dma_start(out=in_bB_sb, in_=in_bB).then_inc(s["ld0"].h, 16)
        P.on("sync", sy_inb)
        inb_tgt = s["ld0"].inc(16)

        for dt_ in range(8 if P2CUT >= 1 else 0):
            def sy_winw(e, dt_=dt_, wprev=s["uev"].v):
                if dt_ > 0:
                    e.wait_ge(s["uev"].h, wprev)
                e.dma_start(out=winw, in_=in_W[dt_ * 128:(dt_ + 1) * 128, :]).then_inc(s["ld0"].h, 16)
            P.on("sync", sy_winw)
            w_tgt = s["ld0"].inc(16)

            def v_cast(e, w=w_tgt, wt=s["xst"].v):
                e.wait_ge(s["ld0"].h, w)
                if wt > 0:
                    e.wait_ge(s["xst"].h, wt)
                e.tensor_copy(winwb, winw).then_inc(s["uev"].h, 1)
            P.on("vector", v_cast)
            cast_tgt = s["uev"].inc()

            def t_wtr(e, dt_=dt_, w=cast_tgt, fr=s["xsfm"].v):
                e.wait_ge(s["uev"].h, w)
                if fr > 0:
                    e.wait_ge(s["xsfm"].h, fr)
                for kt in range(8):
                    ins = e.transpose(pE[:, kt * 128:(kt + 1) * 128],
                                      winwb[:, kt * 128:(kt + 1) * 128], ident[:])
                ins.then_inc(s["xst"].h, 1)
            P.on("tensor", t_wtr)
            wtr_tgt = s["xst"].inc()

            def v_wev(e, dt_=dt_, w=wtr_tgt):
                e.wait_ge(s["xst"].h, w)
                for kt in range(8):
                    ins = e.tensor_copy(
                        in_WT[:, kt * 1024 + dt_ * 128: kt * 1024 + (dt_ + 1) * 128],
                        pE[:, kt * 128:(kt + 1) * 128])
                ins.then_inc(s["xsfm"].h, 1)
            P.on("vector", v_wev)
            s["xsfm"].inc()
        inwt_done = s["xsfm"].v

        SU, SQ, M, MSQ, M2, VAR, VEPS, LNV, RR = (0, 4, 8, 12, 16, 20, 24, 36, 44)
        g4 = lambda off: statrow[:, off:off + 4]
        g1 = lambda off, i: statrow[:, off + i:off + i + 1]

        def VI(ins):
            s["vch"].inc()
            ins.then_inc(s["vch"].h, 1)

        def VW(e):
            e.wait_ge(s["vch"].h, s["vch"].v)

        for ch in range(8 if P2CUT >= 2 else 0):
            xfree_need = s["xfree"].v
            def sy_x1(e, ch=ch, wf=xfree_need):
                if wf > 0:
                    e.wait_ge(s["xfree"].h, wf)
                for lt4 in range(4):
                    lt = ch * 4 + lt4
                    e.dma_start(out=x1chunk[:, lt4 * D:(lt4 + 1) * D],
                                in_=x1_hbm[lt * 128:(lt + 1) * 128, :]).then_inc(s["xld"].h, 16)
            P.on("sync", sy_x1)
            xld_tgt = s["xld"].inc(64)

            xst_prev = s["xst"].v
            def sc_sq(e, w=xld_tgt, wx=xst_prev, ch=ch, wb=inb_tgt):
                if ch == 0:
                    load_act(e, "natural_log_exp_and_others")
                    e.wait_ge(s["ld0"].h, wb)
                e.wait_ge(s["xld"].h, w)
                if wx > 0:
                    e.wait_ge(s["xst"].h, wx)
                for lt4 in range(4):
                    ins = e.activation(out=xsbuf[:], in_=x1chunk[:, lt4 * D:(lt4 + 1) * D],
                                       func=AF.Square, accum_out=g1(SQ, lt4))
                ins.then_inc(s["sq"].h, 1)
            P.on("scalar", sc_sq)
            sq_tgt = s["sq"].inc()

            def v_chain(e, w=xld_tgt, wsq=sq_tgt):
                e.wait_ge(s["xld"].h, w)
                for lt4 in range(4):
                    ins = e.reduce_sum(g1(SU, lt4), x1chunk[:, lt4 * D:(lt4 + 1) * D], axis=AX.X)
                VI(ins)
                e.wait_ge(s["sq"].h, wsq)
                VW(e)
                VI(e.tensor_scalar_mul(g4(M), g4(SU), 1.0 / D))
                VI(e.tensor_scalar_mul(g4(MSQ), g4(SQ), 1.0 / D))
                VW(e)
                VI(e.tensor_mul(g4(M2), g4(M), g4(M)))
                VW(e)
                VI(e.tensor_sub(g4(VAR), g4(MSQ), g4(M2)))
                VW(e)
                e.tensor_scalar_add(g4(VEPS), g4(VAR), EPS).then_inc(s["stats"].h, 1)
            P.on("vector", v_chain)
            st_tgt = s["stats"].inc()

            ln_tgt = s["sq"].inc()
            rsq_tgt = s["sq"].inc()
            def sc_rsq(e, w=st_tgt, wl=ln_tgt):
                e.wait_ge(s["stats"].h, w)
                e.activation(out=g4(LNV), in_=g4(VEPS), func=AF.Ln).then_inc(s["sq"].h, 1)
                e.wait_ge(s["sq"].h, wl)
                e.activation(out=g4(RR), in_=g4(LNV), func=AF.Exp,
                             scale=-0.5).then_inc(s["sq"].h, 1)
            P.on("scalar", sc_rsq)

            def v_chain2(e, w=rsq_tgt, ch=ch):
                e.wait_ge(s["sq"].h, w)
                VI(e.tensor_mul(statrow[:, 48:52], g4(M), g4(RR)))
                e.tensor_scalar_add(srow2[:, ch * 4:(ch + 1) * 4], g4(RR), 1.0)
                VW(e)
                e.tensor_scalar_mul(crow2[:, ch * 4:(ch + 1) * 4], statrow[:, 48:52],
                                    -1.0).then_inc(s["stats"].h, 1)
            P.on("vector", v_chain2)
            st_tgt = s["stats"].inc()

            for lt4 in range(4):
                lt = ch * 4 + lt4
                xst_prev2 = s["xst"].v

                def sc_t2(e, lt4=lt4, ch=ch, w=st_tgt, wx=xst_prev2):
                    e.wait_ge(s["stats"].h, w)
                    if wx > 0:
                        e.wait_ge(s["xst"].h, wx)
                    e.activation(out=xsbuf[:], in_=x1chunk[:, lt4 * D:(lt4 + 1) * D],
                                 func=AF.Identity,
                                 scale=g1(RR, lt4),
                                 bias=crow2[:, ch * 4 + lt4:ch * 4 + lt4 + 1]
                                 ).then_inc(s["xs"].h, 1)
                    e.activation(out=statrow[:, 78:79], in_=statrow[:, 2:3],
                                 func=AF.Identity).then_inc(s["xfree"].h, 1)
                P.on("scalar", sc_t2)
                xs_tgt = s["xs"].inc()
                s["xfree"].inc()

                def t_t2tr(e, w=xs_tgt, fr=s["xsfm"].v, first=(ch == 0 and lt4 == 0),
                           wi=inwt_done):
                    if first:
                        e.wait_ge(s["xsfm"].h, wi)
                    e.wait_ge(s["xs"].h, w)
                    if fr > 0:
                        e.wait_ge(s["xsfm"].h, fr)
                    for kt in range(8):
                        ins = e.transpose(pE[:, kt * 128:(kt + 1) * 128],
                                          xsbuf[:, kt * 128:(kt + 1) * 128], ident[:])
                    ins.then_inc(s["xst"].h, 1)
                P.on("tensor", t_t2tr)
                xst_tgt = s["xst"].inc()

                def v_t2ev(e, lt=lt, w=xst_tgt, wm=s["gmm"].v):
                    e.wait_ge(s["xst"].h, w)
                    if lt >= 2:
                        e.wait_ge(s["gmm"].h, wm - 1)
                    buf = t2fm[lt % 2]
                    for kt in range(8):
                        slot = pE[:, kt * 128:(kt + 1) * 128]
                        rev = bass.AP(slot.tensor, slot.offset + 127,
                                      [[slot.ap[0][0], 128], [-1, 128]])
                        ins = e.tensor_copy(buf[:, kt * 128:(kt + 1) * 128], rev)
                    ins.then_inc(s["xsfm"].h, 1)
                P.on("vector", v_t2ev)
                ev_tgt = s["xsfm"].inc()

                pcd = (pC, pD) if lt % 2 == 0 else (pA, pB)
                def t_inproj(e, lt=lt, w=ev_tgt, uv=s["uev"].v, pcd=pcd):
                    e.wait_ge(s["xsfm"].h, w)
                    if lt >= 2:
                        e.wait_ge(s["uev"].h, uv - 1)
                    buf = t2fm[lt % 2]
                    for kt in range(8):
                        e.matmul(pcd[0][:], buf[:, kt * 128:(kt + 1) * 128],
                                 in_WT[:, kt * 1024: kt * 1024 + 512],
                                 start=(kt == 0), stop=(kt == 7))
                    for kt in range(8):
                        ins = e.matmul(pcd[1][:], buf[:, kt * 128:(kt + 1) * 128],
                                       in_WT[:, kt * 1024 + 512: (kt + 1) * 1024],
                                       start=(kt == 0), stop=(kt == 7))
                    ins.then_inc(s["gmm"].h, 1)
                P.on("tensor", t_inproj)
                mm_tgt = s["gmm"].inc()

                def v_uev(e, lt=lt, w=mm_tgt, pcd=pcd):
                    e.wait_ge(s["gmm"].h, w)
                    c = lt
                    for half, bank in ((0, pcd[0]), (1, pcd[1])):
                        out_ap = bass.AP(u_T[:].tensor,
                                         u_T[:].offset + half * 4 * 4096 + c * 128,
                                         [[u_T[:].ap[0][0], 128], [4096, 4], [1, 128]])
                        in_ps = bass.AP(bank[:].tensor, bank[:].offset,
                                        [[bank[:].ap[0][0], 128], [128, 4], [1, 128]])
                        in_b = bass.AP(in_bB_sb.tensor, in_bB_sb.offset + half * 512,
                                       [[in_bB_sb.ap[0][0], 128], [128, 4], [1, 128]])
                        ins = e.tensor_add(out_ap, in_ps, in_b)
                    ins.then_inc(s["uev"].h, 1)
                P.on("vector", v_uev)
                s["uev"].inc()

        if DEV_STAGE == 2:
            def sy_udump(e, w=s["uev"].v):
                e.wait_ge(s["uev"].h, w)
                e.dma_start(out=uT_hbm, in_=u_T[:]).then_inc(s["x1st"].h, 16)
            P.on("sync", sy_udump)
            s["x1st"].inc(16)

    if DEV_STAGE >= 3:
        barrier(pre_waits=[("vector", "uev", s["uev"].v),
                           ("sync", "x1st", s["x1st"].v),
                           ("tensor", "xsfm", s["xsfm"].v)])

        uT_ap = w1g[:]
        qb_bf = qbuf[:].bitcast(BF16)
        xh_bf = xhalf[:].bitcast(BF16)
        scl_bf3 = sclbuf[:].bitcast(BF16)
        wgb = xs_fm[:]
        t0b = w2buf[:]
        ytst = h2_fm[:, 4096:8192]
        ypm = wtmp[:]
        smre = qb_bf
        smim_bufs = (xs_fm[:], scl_bf3[:, 0:4096])
        lqzr, lqzi = zfm[:, 0:256], zfm[:, 256:512]
        st_re = [sclbuf[:, 2048:2304], sclbuf[:, 2304:2560]]
        st_im = [sclbuf[:, 2560:2816], sclbuf[:, 2816:3072]]
        sctmp = sclbuf[:, 3328:3584]

        def ghre_sl(g, o, n):
            return h2f[:, g * 1024 + o: g * 1024 + o + n]

        def ghim_sl(g, o, n):
            buf = wtmp if g < 4 else xhalf
            return buf[:, (g % 4) * 1024 + o: (g % 4) * 1024 + o + n]

        def VI3(ins):
            s["vch"].inc()
            ins.then_inc(s["vch"].h, 1)

        def VW3(e):
            e.wait_ge(s["vch"].h, s["vch"].v)

        UST = uT_ap.ap[0][0]

        def v_lqz(e):
            VI3(e.tensor_copy(lqzr, sclbuf[:, LQTRE:LQTRE + 256]))
            e.tensor_copy(lqzi, sclbuf[:, LQTIM:LQTIM + 256]).then_inc(s["shf"].h, 1)
        P.on("vector", v_lqz)
        s["shf"].inc()

        prev_gmm = s["gmm"].v
        for g in range(NG):
            wg_tgts = []
            for hg in range(2):
                gate_mm = prev_gmm + hg
                def sy_wgl(e, g=g, hg=hg, wm=gate_mm):
                    if wm > 0:
                        e.wait_ge(s["gmm"].h, wm)
                    e.dma_start(out=wgb, in_=wg_hbm[g][:, hg * 4096:(hg + 1) * 4096]
                                ).then_inc(s["w2ld"].h, 16)
                P.on("sync", sy_wgl)
                wg_tgts.append(s["w2ld"].inc(16))

            round_tgts = []
            for tqB in range(2):
                ev_gate = s["uev"].v
                def t_g(e, g=g, tqB=tqB, wl=wg_tgts[tqB], we=ev_gate):
                    e.wait_ge(s["w2ld"].h, wl)
                    if we > 0:
                        e.wait_ge(s["uev"].h, we)
                    for tqL in range(16):
                        tq = tqB * 16 + tqL
                        for m in range(4):
                            hs = 4 * tq + m
                            hl = hs % 64
                            for comp, bank in ((0, pG), (1, pH)):
                                st = bass.AP(wgb.tensor, wgb.offset + comp * 2048 + hl,
                                             [[wgb.ap[0][0], 128], [64, 32]])
                                mv = bass.AP(uT_ap.tensor, uT_ap.offset + g * 4096 + hs,
                                             [[UST, 128], [128, 32]])
                                ins = e.matmul(bank[32 * m:32 * m + 32,
                                                    tqL * 32:(tqL + 1) * 32],
                                               st, mv, start=True, stop=True,
                                               tile_position=(0, 32 * m))
                    ins.then_inc(s["gmm"].h, 1)
                P.on("tensor", t_g)
                round_tgts.append(s["gmm"].inc())

                def v_gev(e, g=g, tqB=tqB, w=round_tgts[-1]):
                    e.wait_ge(s["gmm"].h, w)
                    e.tensor_copy(ghre_sl(g, tqB * 512, 512), pG[:])
                    e.tensor_copy(ghim_sl(g, tqB * 512, 512),
                                  pH[:]).then_inc(s["uev"].h, 1)
                P.on("vector", v_gev)
                s["uev"].inc()
            prev_gmm = round_tgts[-1]
        gev_done = s["uev"].v

        def v_scan(e, w=gev_done):
            e.wait_ge(s["uev"].h, w)
            e.memset(bass.AP(smre.tensor, smre.offset,
                             [[smre.ap[0][0], 128], [32, 256], [1, 1]]), 0.0)
            for gh in range(2):
                buf = smim_bufs[gh]
                e.memset(bass.AP(buf.tensor, buf.offset,
                                 [[buf.ap[0][0], 128], [32, 128], [1, 1]]), 0.0)
            gre = lambda c: bass.AP(h2f.tensor, h2f.offset + c,
                                    [[h2f.ap[0][0], 128], [1024, 8], [32, 32]])
            gimh = lambda b, c: bass.AP(b[:].tensor, b[:].offset + c,
                                        [[b[:].ap[0][0], 128], [1024, 4], [32, 32]])
            smre_st = lambda c: bass.AP(smre.tensor, smre.offset + c,
                                        [[smre.ap[0][0], 128], [1024, 8], [32, 32]])
            smim_st = lambda gh, c: bass.AP(smim_bufs[gh].tensor, smim_bufs[gh].offset + c,
                                            [[smim_bufs[gh].ap[0][0], 128], [1024, 4], [32, 32]])
            for c in range(31):
                new_r, old_r = st_re[(c + 1) % 2], st_re[c % 2]
                new_i, old_i = st_im[(c + 1) % 2], st_im[c % 2]
                if c == 0:
                    VI3(e.tensor_copy(new_r, gre(0)))
                    VI3(e.tensor_copy(new_i[:, 0:128], gimh(wtmp, 0)))
                    VI3(e.tensor_copy(new_i[:, 128:256], gimh(xhalf, 0)))
                    VW3(e)
                else:
                    VI3(e.tensor_mul(new_r, lqzr, old_r))
                    VI3(e.tensor_mul(sctmp, lqzi, old_i))
                    VW3(e)
                    VI3(e.tensor_sub(new_r, new_r, sctmp))
                    VI3(e.tensor_mul(new_i, lqzr, old_i))
                    VW3(e)
                    VI3(e.tensor_add(new_r, new_r, gre(c)))
                    VI3(e.tensor_mul(sctmp, lqzi, old_r))
                    VW3(e)
                    VI3(e.tensor_add(new_i, new_i, sctmp))
                    VW3(e)
                    VI3(e.tensor_add(new_i[:, 0:128], new_i[:, 0:128], gimh(wtmp, c)))
                    VI3(e.tensor_add(new_i[:, 128:256], new_i[:, 128:256], gimh(xhalf, c)))
                    VW3(e)
                VI3(e.tensor_copy(smre_st(c + 1), new_r))
                VI3(e.tensor_copy(smim_st(0, c + 1), new_i[:, 0:128]))
                ins = e.tensor_copy(smim_st(1, c + 1), new_i[:, 128:256])
                if c == 30:
                    ins.then_inc(s["shf"].h, 1)
                else:
                    VI3(ins)
                    VW3(e)
        P.on("vector", v_scan)
        scan_tgt = s["shf"].inc()

        prev_cnv_all = s["cnv"].v
        prev_yT_last = s["zt"].v
        prev_yst = 0
        t0_load_tgt = {}
        t0_slot_user = {}
        for g in range(NG):
            def sy_wcl(e, g=g, wc=prev_cnv_all, ws=scan_tgt, wp=s["gdma"].v):
                if wp > 0:
                    e.wait_ge(s["gdma"].h, wp)
                e.wait_ge(s["shf"].h, ws)
                if wc > 0:
                    e.wait_ge(s["cnv"].h, wc)
                e.dma_start(out=xh_bf[:, 0:4096],
                            in_=wcre_hbm[:, g * 4096:(g + 1) * 4096]).then_inc(s["gdma"].h, 16)
                e.dma_start(out=xh_bf[:, 4096:8192],
                            in_=wcim_hbm[:, g * 4096:(g + 1) * 4096]).then_inc(s["gdma"].h, 16)
            def emit_t0(qq, gate):
                slot = qq % 2
                def sy_t0(e, g=g, qq=qq, slot=slot, wc=gate):
                    if wc:
                        e.wait_ge(s["cnv"].h, wc)
                    src_ap = bass.AP(kT_hbm.tensor, (4 * g + qq) * 8192 + 32,
                                     [[32, 128], [1, 4096]])
                    e.dma_start(out=t0b[:, slot * 4096:(slot + 1) * 4096],
                                in_=src_ap).then_inc(s["w2ld"].h, 16)
                P.on("sync", sy_t0)
                t0_load_tgt[(g, qq)] = s["w2ld"].inc(16)

            emit_t0(0, t0_slot_user.get(0))
            emit_t0(1, t0_slot_user.get(1))
            P.on("gpsimd", sy_wcl)
            wc_tgt = s["gdma"].inc(32)

            bank_ev = {0: 0, 1: 0}
            yev_tgts = []
            for hB in range(8):
                bank = pC if hB % 2 == 0 else pD
                qq = hB // 2
                slot = qq % 2
                smim_b = smim_bufs[0] if g < 4 else smim_bufs[1]
                smim_base = (g % 4) * 1024
                def t_y(e, g=g, hB=hB, bank=bank, slot=slot, ws=scan_tgt,
                        wt=t0_load_tgt[(g, qq)], wwc=wc_tgt,
                        wb=bank_ev[hB % 2], smim_b=smim_b, smim_base=smim_base):
                    if hB == 0:
                        e.wait_ge(s["shf"].h, ws)
                        e.wait_ge(s["gdma"].h, wwc)
                    e.wait_ge(s["w2ld"].h, wt)
                    if wb > 0:
                        e.wait_ge(s["yev"].h, wb)
                    for hL in range(16):
                        hs = hB * 16 + hL
                        hq = hs % 32
                        m = hs % 4
                        tq = hs // 4
                        out = bank[:, hL * 32:(hL + 1) * 32]
                        st0 = bass.AP(t0b.tensor, t0b.offset + slot * 4096 + hq,
                                      [[t0b.ap[0][0], 128], [32, 128]])
                        mvu = bass.AP(uT_ap.tensor, uT_ap.offset + g * 4096 + hs,
                                      [[UST, 128], [128, 32]])
                        e.matmul(out, st0, mvu, start=True, stop=False)
                        stre = xh_bf[32 * m:32 * m + 32, tq * 128:(tq + 1) * 128]
                        stim = xh_bf[32 * m:32 * m + 32,
                                     4096 + tq * 128: 4096 + (tq + 1) * 128]
                        mre = smre[32 * m:32 * m + 32,
                                   g * 1024 + tq * 32: g * 1024 + (tq + 1) * 32]
                        mim = smim_b[32 * m:32 * m + 32,
                                     smim_base + tq * 32: smim_base + (tq + 1) * 32]
                        e.matmul(out, stre, mre, start=False, stop=False,
                                 tile_position=(32 * m, 0))
                        ins = e.matmul(out, stim, mim, start=False, stop=True,
                                       tile_position=(32 * m, 0))
                    ins.then_inc(s["cnv"].h, 1)
                P.on("tensor", t_y)
                cnv_tgt = s["cnv"].inc()

                def v_yev(e, g=g, hB=hB, bank=bank, w=cnv_tgt,
                          wz=prev_yT_last if hB == 0 else 0):
                    e.wait_ge(s["cnv"].h, w)
                    if wz > 0:
                        e.wait_ge(s["zt"].h, wz)
                    e.tensor_copy(ypm[:, hB * 512:(hB + 1) * 512],
                                  bank[:]).then_inc(s["yev"].h, 1)
                P.on("vector", v_yev)
                bank_ev[hB % 2] = s["yev"].inc()
                yev_tgts.append(bank_ev[hB % 2])
                if hB % 2 == 1:
                    t0_slot_user[qq % 2] = cnv_tgt
                    if qq < 2:
                        emit_t0(qq + 2, cnv_tgt)
            prev_cnv_all = s["cnv"].v
            ypm_done = yev_tgts[-1]

            tr_bank_gate = {0: 0, 1: 0}
            for r in range(8):
                bank = pF if r % 2 == 0 else pA
                def t_ytr(e, g=g, r=r, bank=bank, w=ypm_done,
                          wg2=tr_bank_gate[r % 2]):
                    e.wait_ge(s["yev"].h, w)
                    if wg2 > 0:
                        e.wait_ge(s["h2"].h, wg2)
                    for j in range(4):
                        c = r * 4 + j
                        inap = bass.AP(ypm.tensor, ypm.offset + c,
                                       [[ypm.ap[0][0], 128], [32, 128]])
                        ins = e.transpose(bank[:, j * 128:(j + 1) * 128],
                                          inap, ident32[:])
                    ins.then_inc(s["zt"].h, 1)
                P.on("tensor", t_ytr)
                ytr_tgt = s["zt"].inc()

                def sc_gelu(e, g=g, r=r, bank=bank, w=ytr_tgt, wst=prev_yst):
                    if g == 0 and r == 0:
                        load_act(e, "gelu_and_others")
                    if r == 0 and wst > 0:
                        e.wait_ge(s["ld0"].h, wst)
                    e.wait_ge(s["zt"].h, w)
                    for j in range(4):
                        c = r * 4 + j
                        ins = e.activation(out=ytst[:, c * 128:(c + 1) * 128],
                                           in_=bank[:, j * 128:(j + 1) * 128],
                                           func=AF.Gelu)
                    ins.then_inc(s["h2"].h, 1)
                P.on("scalar", sc_gelu)
                tr_bank_gate[r % 2] = s["h2"].inc()
            prev_yT_last = s["zt"].v
            gelu_done = s["h2"].v

            def sy_yst(e, g=g, w=gelu_done):
                e.wait_ge(s["h2"].h, w)
                e.dma_start(out=yhm_hbm[g * 128:(g + 1) * 128, :],
                            in_=ytst).then_inc(s["ld0"].h, 16)
            P.on("sync", sy_yst)
            prev_yst = s["ld0"].inc(16)
        conv_store_done = prev_yst

    if DEV_STAGE >= 4 and DEV_STAGE != 25:
        barrier(pre_waits=[("sync", "ld0", s["ld0"].v),
                           ("vector", "shf", s["shf"].v),
                           ("scalar", "h2", s["h2"].v),
                           ("tensor", "h2", s["h2"].v)])
        qb_bf3 = qbuf[:].bitcast(BF16)
        gluw_sb = h2_fm[:]
        outw_sb = w2buf[:]
        ybuf = xs_fm[:]
        y2buf = qb_bf3[:, 0:4096]
        x1ch3 = sclbuf[:, 0:4096]
        atmp = wtmp[:, 0:512]
        sgtmp = zfm[:, 0:512]
        xhst = wtmp[:, 1024:2048]
        outb_sb = wtmp[:, 2048:3072]
        x2st = [xhalf[:, 0:1024], xhalf[:, 1024:2048]]
        x2_store_tgt = {0: 0, 1: 0}

        def sy_w1rl(e):
            for gi in range(NG):
                e.dma_start(out=w1g[:, gi * FF:(gi + 1) * FF],
                            in_=w1g_hbm[:, gi * FF:(gi + 1) * FF]).then_inc(s["ld0"].h, 16)
        P.on("sync", sy_w1rl)
        w1rl_tgt = s["ld0"].inc(16 * 8)

        def sy_p3w(e):
            e.dma_start(out=gluw_sb, in_=gluWT_hbm).then_inc(s["ld0"].h, 16)
            e.dma_start(out=outw_sb, in_=outWT_hbm).then_inc(s["ld0"].h, 16)
            e.dma_start(out=outb_sb, in_=out_bB).then_inc(s["ld0"].h, 16)
        P.on("sync", sy_p3w)
        p3w_tgt = s["ld0"].inc(48)

        for lch in range(8):
            zdone = s["zev"].v
            def sy_yld(e, lch=lch, wz=zdone):
                if wz > 0:
                    e.wait_ge(s["zev"].h, wz)
                for gi in range(NG):
                    e.dma_start(out=ybuf[:, gi * 512:(gi + 1) * 512],
                                in_=yhm_hbm[gi * 128:(gi + 1) * 128,
                                            lch * 512:(lch + 1) * 512]
                                ).then_inc(s["xld"].h, 16)
            P.on("sync", sy_yld)
            yld_tgt = s["xld"].inc(16 * 8)

            xapp_done = s["xs"].v
            def sy_x1ld(e, lch=lch, wx=xapp_done):
                if wx > 0:
                    e.wait_ge(s["xs"].h, wx)
                for lt4 in range(4):
                    lt = lch * 4 + lt4
                    e.dma_start(out=x1ch3[:, lt4 * 1024:(lt4 + 1) * 1024],
                                in_=x1_hbm[lt * 128:(lt + 1) * 128, :]
                                ).then_inc(s["xld"].h, 16)
            P.on("sync", sy_x1ld)
            x1ld_tgt = s["xld"].inc(64)

            for ot in range(8):
                y2_done = s["zev"].v
                def t_z(e, lch=lch, ot=ot, w=yld_tgt, wp=p3w_tgt, wy=y2_done):
                    if ot == 0:
                        e.wait_ge(s["ld0"].h, wp)
                        e.wait_ge(s["xld"].h, w)
                    if wy > 0:
                        e.wait_ge(s["zev"].h, wy)
                    for half, bank in ((0, pA), (1, pB)):
                        oti = ot + 8 * half
                        for gi in range(NG):
                            ins = e.matmul(bank[:],
                                           gluw_sb[:, oti * 1024 + gi * 128:
                                                   oti * 1024 + (gi + 1) * 128],
                                           ybuf[:, gi * 512:(gi + 1) * 512],
                                           start=(gi == 0), stop=(gi == 7))
                    ins.then_inc(s["fc1"].h, 1)
                P.on("tensor", t_z)
                z_tgt = s["fc1"].inc()

                def sc_sig(e, lch=lch, ot=ot, w=z_tgt):
                    if lch == 0 and ot == 0:
                        load_act(e, "sigmoid_and_others")
                    e.wait_ge(s["fc1"].h, w)
                    e.activation(out=sgtmp, in_=pB[:], func=AF.Sigmoid,
                                 bias=glub_sb[:, ot + 8:ot + 9])
                    e.activation(out=atmp, in_=pA[:], func=AF.Identity,
                                 bias=glub_sb[:, ot:ot + 1]).then_inc(s["sq"].h, 1)
                P.on("scalar", sc_sig)
                sig_tgt = s["sq"].inc()

                def v_y2(e, ot=ot, w=sig_tgt):
                    e.wait_ge(s["sq"].h, w)
                    e.tensor_mul(y2buf[:, ot * 512:(ot + 1) * 512], atmp,
                                 sgtmp).then_inc(s["zev"].h, 1)
                P.on("vector", v_y2)
                s["zev"].inc()
            y2_all = s["zev"].v

            for lt4 in range(4):
                lt = lch * 4 + lt4
                def sc_xapp(e, lch=lch, lt4=lt4, lt=lt, w=x1ld_tgt, wy=s["x1rdy"].v):
                    e.wait_ge(s["xld"].h, w)
                    if wy > 0:
                        e.wait_ge(s["x1rdy"].h, wy)
                    e.activation(out=xhst, in_=x1ch3[:, lt4 * 1024:(lt4 + 1) * 1024],
                                 func=AF.Identity, scale=srow2[:, lt:lt + 1],
                                 bias=crow2[:, lt:lt + 1]).then_inc(s["xs"].h, 1)
                P.on("scalar", sc_xapp)
                xapp_tgt = s["xs"].inc()

                op_tgts = []
                for dh in range(2):
                    bank = pC if dh == 0 else pD
                    x2done = s["x1rdy"].v
                    def t_op(e, lt4=lt4, dh=dh, bank=bank, wy=y2_all, wx=x2done):
                        if lt4 == 0 and dh == 0:
                            e.wait_ge(s["zev"].h, wy)
                        if wx > 0:
                            e.wait_ge(s["x1rdy"].h, wx)
                        for ob in range(NG):
                            mvw = bass.AP(outw_sb.tensor,
                                          outw_sb.offset + 4 * dh * 1024 + ob * 128,
                                          [[outw_sb.ap[0][0], 128], [1024, 4], [1, 128]])
                            ins = e.matmul(bank[:],
                                           y2buf[:, ob * 512 + lt4 * 128:
                                                 ob * 512 + (lt4 + 1) * 128],
                                           mvw, start=(ob == 0), stop=(ob == 7))
                        ins.then_inc(s["fc2"].h, 1)
                    P.on("tensor", t_op)
                    op_tgts.append(s["fc2"].inc())

                def v_x2(e, lt4=lt4, w=op_tgts, wx=xapp_tgt,
                         wst=x2_store_tgt[lt4 % 2]):
                    e.wait_ge(s["xs"].h, wx)
                    if wst > 0:
                        e.wait_ge(s["x1st"].h, wst)
                    x2t = x2st[lt4 % 2]
                    e.wait_ge(s["fc2"].h, w[0])
                    VI3(e.tensor_add(x2t[:, 0:512], pC[:], outb_sb[:, 0:512]))
                    e.wait_ge(s["fc2"].h, w[1])
                    VI3(e.tensor_add(x2t[:, 512:1024], pD[:], outb_sb[:, 512:1024]))
                    VW3(e)
                    VI3(e.tensor_add(x2t[:, 0:512], x2t[:, 0:512], xhst[:, 0:512]))
                    e.tensor_add(x2t[:, 512:1024], x2t[:, 512:1024],
                                 xhst[:, 512:1024]).then_inc(s["x1rdy"].h, 1)
                P.on("vector", v_x2)
                x2r_tgt = s["x1rdy"].inc()

                def sy_x2(e, lt=lt, lt4=lt4, w=x2r_tgt):
                    e.wait_ge(s["x1rdy"].h, w)
                    e.dma_start(out=x2_hbm[lt * 128:(lt + 1) * 128, :],
                                in_=x2st[lt4 % 2]).then_inc(s["x1st"].h, 16)
                P.on("sync", sy_x2)
                x2_store_tgt[lt4 % 2] = s["x1st"].inc(16)

        if DEV_STAGE == 4:
            pass

    if DEV_STAGE >= 5 and DEV_STAGE != 25:
        barrier(pre_waits=[("sync", "x1st", s["x1st"].v),
                           ("vector", "x1rdy", s["x1rdy"].v),
                           ("tensor", "x1rdy", s["x1rdy"].v)])

        emit_ffn(x2_hbm, out_ext, ("ld0", w1rl_tgt))

    def sy_fin(e, w=s["x1st"].v, wl=s["ld0"].v):
        e.wait_ge(s["x1st"].h, w)
        e.wait_ge(s["ld0"].h, wl)
    P.on("sync", sy_fin)

    with nc.Block() as blk:
        blk.sync(lambda e: [f(e) for f in P.ops["sync"]])
        blk.gpsimd(lambda e: [f(e) for f in P.ops["gpsimd"]])
        blk.vector(lambda e: [f(e) for f in P.ops["vector"]])
        blk.scalar(lambda e: [f(e) for f in P.ops["scalar"]])
        blk.tensor(lambda e: [f(e) for f in P.ops["tensor"]])
    es_p01.close()
    es.close()
    return nc


_cached = {}


def kernel(**inputs):
    nc = _cached.get("nc")
    if nc is None:
        nc = build_program()
        _cached["nc"] = nc

    f32 = lambda a: np.ascontiguousarray(np.asarray(a, dtype=np.float32))
    i32 = lambda a: np.ascontiguousarray(np.asarray(a, dtype=np.int32))
    x = f32(inputs["L0_attn_x"])

    fc1_scales = f32(inputs["L0_ffn_fc1_scales"])
    fc2_scales = f32(inputs["L0_ffn_fc2_scales"])
    fc2_q = i32(inputs["L0_ffn_fc2_q"])
    fc2_qc = np.ascontiguousarray(
        fc2_q.reshape(32, 128, 8, 128).transpose(2, 1, 0, 3).reshape(8, 128, 32 * 128))
    fc2_scales_c = np.ascontiguousarray(np.broadcast_to(
        fc2_scales.reshape(32, 8, 128).transpose(1, 0, 2)[:, None, :, :],
        (8, 128, 32, 128)).reshape(8, 128, FF))
    fc1_scales_b = np.ascontiguousarray(
        np.broadcast_to(fc1_scales[:, None, :], (8, 128, FF)))

    col = lambda a, n: np.ascontiguousarray(f32(a).reshape(n, 128).T)
    bcast = lambda a: np.ascontiguousarray(
        np.broadcast_to(f32(a)[None, :], (128, f32(a).shape[0])))

    grid = lambda a: np.ascontiguousarray(
        f32(a).reshape(8, 128, 32).transpose(1, 0, 2).reshape(128, 256))
    C = f32(inputs["L0_attn_C"])
    common = dict(
        in_W=f32(inputs["L0_attn_in_W"]),
        in_bB=bcast(inputs["L0_attn_in_b"]),
        dt_col=col(inputs["L0_attn_log_dt"], 8),
        dsk_col=col(inputs["L0_attn_D_skip"], 8),
        lar_g=grid(inputs["L0_attn_log_A_real"]),
        aim_g=grid(inputs["L0_attn_A_imag"]),
        c_re_g=grid(C[..., 0]),
        c_im_g=grid(C[..., 1]),
        glu_W=f32(inputs["L0_attn_glu_W"]),
        glu_b_col=col(inputs["L0_attn_glu_b"], 16),
        out_W=f32(inputs["L0_attn_out_W"]),
        out_bB=bcast(inputs["L0_attn_out_b"]),
        gamma_col=col(inputs["L0_ffn_ln_gamma"], 8),
        beta_col=col(inputs["L0_ffn_ln_beta"], 8),
        fc1_q=i32(inputs["L0_ffn_fc1_q"]),
        fc1_scales_b=fc1_scales_b,
        fc1_bias_col=col(inputs["L0_ffn_fc1_bias"], 32),
        fc2_qc=fc2_qc,
        fc2_scales_c=fc2_scales_c,
        fc2_bias_col=col(inputs["L0_ffn_fc2_bias"], 8),
    )

    in_maps = []
    for b in range(NCORES):
        m = dict(common)
        m["x"] = np.ascontiguousarray(x[b])
        in_maps.append(m)

    res = run_bass_kernel_spmd(nc, in_maps, core_ids=list(range(NCORES)))
    if DEV_STAGE == 25:
        return {k: np.asarray(res.results[0][k]) for k in
                ("kT_hbm", "wg_hbm", "wcre_hbm", "wcim_hbm", "lamq_hbm",
                 "gluWT_hbm", "outWT_hbm")}
    key = "out" if DEV_STAGE == 5 else {1: "dbg_x1", 2: "dbg_u", 3: "dbg_y", 4: "dbg_x2"}[DEV_STAGE]
    outs = [np.asarray(res.results[b][key]) for b in range(NCORES)]
    return np.stack(outs, 0)
